# version 11
# baseline (speedup 1.0000x reference)
import numpy as np
from contextlib import ExitStack
import concourse.bass as bass
import concourse.mybir as mybir
from concourse.bass_utils import run_bass_kernel_spmd

F32 = mybir.dt.float32
F16 = mybir.dt.float16
BF16 = mybir.dt.bfloat16
AF = mybir.ActivationFunctionType
ALU = mybir.AluOpType
AX = mybir.AxisListType

MAGIC = 12582912.0
NC = 8
D = 2048
KD = D // 128
I = 8192
KI = I // 128
TC = 1024
TT = 512
NT = TC // TT
MODS = 6 * D // NC
FOLD_ORDERS = [
    [12 * r + j for r in range(8) for j in (0, 1)],
    [12 * r + j for r in range(8) for j in (2, 3)],
    [54, 19, 69, 78, 53, 46, 17, 42, 4, 28, 64, 44, 33, 21, 55, 16, 81, 66, 56, 45, 79, 90, 29, 71, 41, 57, 70, 32, 67, 94, 76, 6, 20, 52, 31, 82, 40, 30, 58, 35, 5, 10, 22, 80, 77, 68, 65, 9, 83, 11, 93, 91, 92, 88, 89, 18, 43, 95, 47, 23, 34, 8, 7, 59],
]
DEBUG = False

_CACHED = {}


def build_nc():
    nc = bass.Bass("TRN2", target_bir_lowering=False, debug=False)

    x_in = nc.dram_tensor("x_in", [D, TC], F32, kind="ExternalInput").ap()
    wi_d = nc.dram_tensor("wi", [D, D], F32, kind="ExternalInput").ap()
    wf_d = nc.dram_tensor("wf", [D, D], F32, kind="ExternalInput").ap()
    wg_d = nc.dram_tensor("wg", [D, D], F32, kind="ExternalInput").ap()
    woT = nc.dram_tensor("woT", [D, D], F32, kind="ExternalInput").ap()
    wmod_sl = nc.dram_tensor("wmod_sl", [D, MODS], F32, kind="ExternalInput").ap()
    tsl_g = nc.dram_tensor("tsl_g", [D // NC, I], F32, kind="ExternalInput").ap()
    tsl_u = nc.dram_tensor("tsl_u", [D // NC, I], F32, kind="ExternalInput").ap()
    tsl_d = nc.dram_tensor("tsl_d", [I // NC, D], F32, kind="ExternalInput").ap()
    c_pk = nc.dram_tensor("c_pk", [128, KD], F32, kind="ExternalInput").ap()
    si_i = nc.dram_tensor("si", [128, KD], F32, kind="ExternalInput").ap()
    sf_i = nc.dram_tensor("sf", [128, KD], F32, kind="ExternalInput").ap()
    sg_i = nc.dram_tensor("sg", [128, KD], F32, kind="ExternalInput").ap()
    so_i = nc.dram_tensor("so", [128, KD], F32, kind="ExternalInput").ap()
    gnw_i = nc.dram_tensor("gnw", [128, KD], F32, kind="ExternalInput").ap()
    lnw_i = nc.dram_tensor("lnw", [128, KI], F32, kind="ExternalInput").ap()
    cm0_i = nc.dram_tensor("cm0", [128, NC], F32, kind="ExternalInput").ap()
    cm1_i = nc.dram_tensor("cm1", [128, NC], F32, kind="ExternalInput").ap()
    id_i = nc.dram_tensor("ident", [128, 128], F32, kind="ExternalInput").ap()

    out_t = nc.dram_tensor("out_t", [D, TC], F32, kind="ExternalOutput").ap()

    spill_kind = "ExternalOutput" if DEBUG else "Internal"
    ag1_in = nc.dram_tensor("ag1_in", [1, MODS + 16], F32, kind="Internal").ap()
    ag1_out = nc.dram_tensor("ag1_out", [NC, MODS + 16], F32, kind="Internal").ap()
    tgin = nc.dram_tensor("tgin", [D // NC, I], BF16, kind="Internal")
    tgout = nc.dram_tensor("tgout", [D, I], BF16, kind="Internal")
    tuin = nc.dram_tensor("tuin", [D // NC, I], BF16, kind="Internal")
    tuout = nc.dram_tensor("tuout", [D, I], BF16, kind="Internal")
    tdin = nc.dram_tensor("tdin", [I // NC, D], BF16, kind="Internal")
    tdout = nc.dram_tensor("tdout", [I, D], BF16, kind="Internal")
    sum_in = nc.dram_tensor("sum_in", [2 * KD, 128], F32, kind="Internal").ap()
    sum_out = nc.dram_tensor("sum_out", [NC * 2 * KD, 128], F32, kind="Internal").ap()
    h_sp = nc.dram_tensor("h_sp", [D, TC], F16, kind=spill_kind).ap()
    p_sp = nc.dram_tensor("p_sp", [D, TC], F16, kind=spill_kind).ap()
    g_sp = nc.dram_tensor("g_sp", [D, TC], F16, kind=spill_kind).ap()
    x1_sp = nc.dram_tensor("x1_sp", [D, TC], F32, kind=spill_kind).ap()
    hf_sp = nc.dram_tensor("hf_sp", [I, TC], F16, kind=spill_kind).ap()
    if DEBUG:
        dbg_o16 = nc.dram_tensor("dbg_o16", [D, TC], F16, kind="ExternalOutput").ap()
        dbg_qx = nc.dram_tensor("dbg_qx", [D, TC], BF16, kind="ExternalOutput").ap()
        dbg_carry = nc.dram_tensor("dbg_carry", [128, KD], F32, kind="ExternalOutput").ap()
        dbg_rows = nc.dram_tensor("dbg_rows", [3, TC], F32, kind="ExternalOutput").ap()
        dbg_att = nc.dram_tensor("dbg_att", [D, TC], F32, kind="ExternalOutput").ap()
        dbg_adc = nc.dram_tensor("dbg_adc", [128, 96 + 2 * KD], F32, kind="ExternalOutput").ap()
        dbg_xf = nc.dram_tensor("dbg_xf", [D, TC], F16, kind="ExternalOutput").ap()
        dbg_fi = nc.dram_tensor("dbg_fi", [2 * 128, TC], F32, kind="ExternalOutput").ap()
        dbg_qh = nc.dram_tensor("dbg_qh", [I, TT], BF16, kind="ExternalOutput").ap()
        dbg_r2 = nc.dram_tensor("dbg_r2", [4, TC], F32, kind="ExternalOutput").ap()
        dbg_td = nc.dram_tensor("dbg_td", [I // NC, D], BF16, kind="ExternalOutput").ap()
        dbg_tg = nc.dram_tensor("dbg_tg", [D // NC, I], BF16, kind="ExternalOutput").ap()

    plan = {k: [] for k in ("sync", "gpsimd", "tensor", "vector", "scalar")}
    sems = {}
    cnt = {}

    with ExitStack() as top:
        E = top.enter_context
        for name in ("qS", "qG", "tM", "vV", "aA", "cC"):
            sems[name] = E(nc.semaphore(name))
            cnt[name] = 0

        def em(eng, fn):
            plan[eng].append(fn)

        def dma(eng, out, in_, q, slow=False):
            cnt[q] += 16
            s = sems[q]
            if slow:
                em(eng, lambda e, out=out, in_=in_, s=s: e.dma_start(
                    out=out, in_=in_, allow_slow_non_contiguous=True).then_inc(s, 16))
            else:
                em(eng, lambda e, out=out, in_=in_, s=s: e.dma_start(out=out, in_=in_).then_inc(s, 16))
            return cnt[q]

        def wait(eng, q, v):
            if v is None or v <= 0:
                return
            s = sems[q]
            em(eng, lambda e, s=s, v=v: e.wait_ge(s, v))

        def op(eng, fn, q=None, k=1):
            if q is None:
                em(eng, lambda e, f=fn: f(e))
                return None
            cnt[q] += k
            s = sems[q]
            em(eng, lambda e, f=fn, s=s, k=k: f(e).then_inc(s, k))
            return cnt[q]

        def fence(eng):
            q = {"vector": "vV", "scalar": "aA"}[eng]
            return op(eng, lambda e: e.engine_nop(), q)

        def coll(src, dst):
            return op("gpsimd", lambda g, s=src, d=dst: g.collective_compute(
                "AllGather", ALU.bypass, replica_groups=[list(range(NC))],
                ins=[s], outs=[d]), "cC")

        def barrier_dma():
            for e in ("sync", "gpsimd"):
                wait(e, "tM", cnt["tM"])
                wait(e, "vV", cnt["vV"])
                wait(e, "aA", cnt["aA"])

        ones16 = E(nc.sbuf_tensor("ones16", [128, 128], F16))
        ones32 = E(nc.sbuf_tensor("ones32", [128, 128], F32))
        ones1 = E(nc.sbuf_tensor("ones1f", [1, 128], F32))
        onescol = E(nc.sbuf_tensor("onescol", [128, 1], F32))
        ident = E(nc.sbuf_tensor("identt", [128, 128], F32))
        sivec = E(nc.sbuf_tensor("sivec", [128, KD], F32))
        sfvec = E(nc.sbuf_tensor("sfvec", [128, KD], F32))
        sgvec = E(nc.sbuf_tensor("sgvec", [128, KD], F32))
        gnwso = E(nc.sbuf_tensor("gnwso", [128, KD], F32))
        lnwv = E(nc.sbuf_tensor("lnwv", [128, KI], F32))
        lnw2 = E(nc.sbuf_tensor("lnw2", [128, KI], F32))
        cm0 = E(nc.sbuf_tensor("cm0s", [128, NC], F32))
        cm1 = E(nc.sbuf_tensor("cm1s", [128, NC], F32))
        adc = E(nc.sbuf_tensor("adc", [128, 96], F32))
        ca256 = E(nc.sbuf_tensor("ca256", [128, KD], F32))
        sa256 = E(nc.sbuf_tensor("sa256", [128, KD], F32))
        cb1 = E(nc.sbuf_tensor("cb1", [128, KD], F32))
        sb_c = E(nc.sbuf_tensor("sb_c", [128, KD], F32))
        gac = E(nc.sbuf_tensor("gac", [128, KD], F32))
        gbc = E(nc.sbuf_tensor("gbc", [128, KD], F32))
        swv = E(nc.sbuf_tensor("swv", [128, 8], F32))
        carry = E(nc.sbuf_tensor("carry", [128, KD], F32))
        scv = E(nc.sbuf_tensor("scv", [128, KD], F32))
        tiny = E(nc.sbuf_tensor("tiny", [128, KD], F32))
        rowA = E(nc.sbuf_tensor("rowA", [1, TC], F32))
        rowB = E(nc.sbuf_tensor("rowB", [1, TC], F32))
        dqrow = E(nc.sbuf_tensor("dqrow", [1, TC], F32))
        c2row = E(nc.sbuf_tensor("c2row", [1, TC], F32))
        dq2row = E(nc.sbuf_tensor("dq2row", [1, TC], F32))
        zeros = E(nc.sbuf_tensor("zeros", [128, TT], F32))
        dqb = E(nc.sbuf_tensor("dqb", [128, TC], F32))
        c2b = E(nc.sbuf_tensor("c2b", [128, TC], F32))
        dq2b = E(nc.sbuf_tensor("dq2b", [128, TC], F32))
        tokm = E(nc.sbuf_tensor("tokm", [128, 16], F32))
        tokr = E(nc.sbuf_tensor("tokr", [16, 128], F32))
        gnwtmp = E(nc.sbuf_tensor("gnwtmp", [128, KD], F32))
        epsc = E(nc.sbuf_tensor("epsc", [128, 2], F32))

        pb = [E(nc.psum_tensor(f"pb{i}", [128, 512], F32)) for i in range(8)]
        fsrc = E(nc.sbuf_tensor("fsrc", [128, 16], F32))
        fdst = E(nc.sbuf_tensor("fdst", [128, 16], F32))
        NFR = 12
        fring = {q: [E(nc.semaphore(f"fr_{q}_{i}")) for i in range(NFR)]
                 for q in ("sync", "gpsimd")}
        fr_idx = {"sync": 0, "gpsimd": 0}
        fr_cnt = {}

        def qfence(eng):
            i = fr_idx[eng] % NFR
            fr_idx[eng] += 1
            key = (eng, i)
            fr_cnt[key] = fr_cnt.get(key, 0) + 16
            sm = fring[eng][i]
            em(eng, lambda e, sm=sm: e.dma_start(out=fdst[:], in_=fsrc[:]).then_inc(sm, 16))
            return (eng, i, fr_cnt[key])

        def wfence(waiter, tok):
            eng, i, v = tok
            em(waiter, lambda e, sm=fring[eng][i], v=v: e.wait_ge(sm, v))

        op("vector", lambda v: v.memset(ones16[:], 1.0))
        op("vector", lambda v: v.memset(ones32[:], 1.0))
        op("vector", lambda v: v.memset(ones1[:], 1.0))
        op("vector", lambda v: v.memset(onescol[:], 1.0))
        op("vector", lambda v: v.memset(zeros[:], 0.0))
        op("vector", lambda v: v.memset(epsc[:, 0:1], 1e-5))
        op("vector", lambda v: v.memset(epsc[:, 1:2], 1e-6))
        q_cpk = dma("sync", tiny[:], c_pk, "qS")
        q_small = q_cpk
        for dst, src in ((sivec, si_i), (sfvec, sf_i), (sgvec, sg_i),
                         (gnwso, so_i), (lnwv, lnw_i), (cm0, cm0_i), (cm1, cm1_i),
                         (ident, id_i)):
            q_small = dma("sync", dst[:], src, "qS")
        q_small = dma("sync", gnwtmp[:], gnw_i, "qS")

        f_small = qfence("sync")
        wait("scalar", "qS", q_small)
        wfence("scalar", f_small)
        a_siluc = op("scalar", lambda s: s.activation(scv[:], tiny[:], AF.Silu), "aA")
        wait("vector", "qS", q_small)
        wfence("vector", f_small)
        op("vector", lambda v: v.tensor_mul(gnwso[:], gnwso[:], gnwtmp[:]))
        op("vector", lambda v: v.tensor_mul(lnw2[:], lnwv[:], lnwv[:]))
        op("vector", lambda v: v.drain())

        def part_max_row(src_t, width, pbank, rowdst, v_ready):
            nb = width // 128
            wait("tensor", "vV", v_ready)
            for b in range(nb):
                t_tr = op("tensor", lambda t, b=b: t.transpose(
                    pb[pbank][:, 0:128], src_t[:, 128 * b:128 * (b + 1)], ident[:]), "tM")
                wait("vector", "tM", t_tr)
                op("vector", lambda v, b=b: v.tensor_reduce(
                    tokm[:, b:b + 1], pb[pbank][:, 0:128], AX.X, ALU.max))
                vm = fence("vector")
                if b < nb - 1:
                    wait("tensor", "vV", vm)
            op("vector", lambda v: v.drain())
            v_tok = fence("vector")
            wait("tensor", "vV", v_tok)
            t_tb = op("tensor", lambda t, nb=nb: t.transpose(
                pb[pbank][0:nb, 0:128], tokm[:, 0:nb], ident[:]), "tM")
            wait("vector", "tM", t_tb)
            op("vector", lambda v, nb=nb: v.tensor_copy(tokr[0:nb, 0:128], pb[pbank][0:nb, 0:128]))
            v_tb = fence("vector")
            wait("gpsimd", "vV", v_tb)
            q_row = 0
            for b in range(nb):
                q_row = dma("gpsimd", rowdst[0:1, 128 * b:128 * (b + 1)], tokr[b:b + 1, :], "qG")
            f_row = qfence("gpsimd")
            wait("vector", "qG", q_row)
            wfence("vector", f_row)
            return fence("vector")

        pxf = ExitStack()
        xf16 = pxf.enter_context(nc.sbuf_tensor("xf16", [128, KD * TC], F16))
        pln = ExitStack()
        EL = pln.enter_context
        xint = EL(nc.sbuf_tensor("xint", [128, KD * TT], F32))
        xsq32 = EL(nc.sbuf_tensor("xsq32", [128, KD * TT], F32))
        mrep = EL(nc.sbuf_tensor("mrep", [128, TT], F32))
        rstd = EL(nc.sbuf_tensor("rstd", [128, TT], F32))
        vtmp = EL(nc.sbuf_tensor("vtmp", [128, TT], F32))
        ttmp = EL(nc.sbuf_tensor("ttmp", [128, TT], F32))
        ln1 = {"f_x": None, "q_x": None, "a_sq": None, "t_stats": None,
               "a_rstd": None, "v_var": None, "v_ln": None}

        def ln1_loads(n):
            q_x = 0
            for k in range(KD):
                q_x = dma("sync", xint[:, TT * k:TT * (k + 1)],
                          x_in[128 * k:128 * (k + 1), TT * n:TT * (n + 1)], "qS")
            ln1["q_x"] = q_x
            ln1["f_x"] = qfence("sync")

        def ln1_sq(n):
            wait("scalar", "qS", ln1["q_x"])
            wfence("scalar", ln1["f_x"])
            ln1["a_sq"] = op("scalar", lambda s: s.activation(xsq32[:], xint[:], AF.Square), "aA")

        def ln1_stats(n):
            wait("tensor", "qS", ln1["q_x"])
            wfence("tensor", ln1["f_x"])
            if n > 0:
                wait("tensor", "vV", ln1["v_var"])
            for k in range(KD):
                op("tensor", lambda t, k=k: t.matmul(
                    pb[6][:], ones32[:], xint[:, TT * k:TT * (k + 1)],
                    start=(k == 0), stop=(k == KD - 1)))
            wait("tensor", "aA", ln1["a_sq"])
            t_stats = None
            for k in range(KD):
                def mm(t, k=k):
                    return t.matmul(pb[7][:], ones32[:], xsq32[:, TT * k:TT * (k + 1)],
                                    start=(k == 0), stop=(k == KD - 1))
                t_stats = op("tensor", mm, "tM") if k == KD - 1 else op("tensor", mm)
            ln1["t_stats"] = t_stats

        def ln1_rows(n):
            wait("vector", "tM", ln1["t_stats"])
            op("vector", lambda v: v.tensor_scalar(mrep[:], pb[6][:], 1.0 / D, None, ALU.mult))
            op("vector", lambda v: v.tensor_scalar(vtmp[:], pb[7][:], 1.0 / D, None, ALU.mult))
            op("vector", lambda v: v.scalar_tensor_tensor(ttmp[:], mrep[:], -1.0, mrep[:],
                                                          ALU.mult, ALU.mult))
            op("vector", lambda v: v.tensor_add(vtmp[:], vtmp[:], ttmp[:]))
            ln1["v_var"] = fence("vector")
            wait("scalar", "vV", ln1["v_var"])
            op("scalar", lambda s: s.activation(ttmp[:], vtmp[:], AF.Ln, bias=epsc[:, 0:1]))
            ln1["a_rstd"] = op("scalar", lambda s: s.activation(rstd[:], ttmp[:], AF.Exp, scale=-0.5), "aA")

        def ln1_chunks(n):
            wait("vector", "aA", ln1["a_rstd"])
            wait("vector", "qS", ln1["q_x"])
            wfence("vector", ln1["f_x"])
            for k in range(KD):
                xs = xint[:, TT * k:TT * (k + 1)]
                dstf = xf16[:, TC * k + TT * n: TC * k + TT * (n + 1)]
                op("vector", lambda v, xs=xs: v.tensor_sub(ttmp[:], xs, mrep[:]))
                op("vector", lambda v: v.tensor_mul(ttmp[:], ttmp[:], rstd[:]))
                op("vector", lambda v, k=k: v.tensor_scalar(
                    ttmp[:], ttmp[:], ca256[:, k:k + 1], sa256[:, k:k + 1], ALU.mult, ALU.add))
                op("vector", lambda v: v.tensor_scalar(
                    ttmp[:], ttmp[:], MAGIC, -MAGIC, ALU.add, ALU.add))
                op("vector", lambda v, d=dstf: v.tensor_scalar(
                    d, ttmp[:], 1.0 / 256.0, None, ALU.mult))
            ln1["v_ln"] = fence("vector")

        ln1_loads(0)
        p01 = ExitStack()
        E0 = p01.enter_context
        wmt = [E0(nc.sbuf_tensor("wmt0", [128, MODS], F32))] * 2
        modrow = E0(nc.sbuf_tensor("modrow", [1, MODS], F32))
        sl_t = [E0(nc.sbuf_tensor(f"slt{i}", [128, 2048], F32)) for i in range(2)]
        partial = E0(nc.sbuf_tensor("partial", [128, 16], F32))
        psum1 = E0(nc.sbuf_tensor("psum1", [128, 4], F32))
        sumrow = E0(nc.sbuf_tensor("sumrow", [1, 12], F32))
        sumg = E0(nc.sbuf_tensor("sumg", [128, 4], F32))
        slotrow = E0(nc.sbuf_tensor("slotrow", [1, 96], F32))
        modst = E0(nc.sbuf_tensor("modst", [96, 128], F32))
        mrow = E0(nc.sbuf_tensor("mrow", [1, 12], F32))
        tern_t = [E0(nc.sbuf_tensor(f"tern{i}", [128, 2048], BF16)) for i in range(2)]

        NMOD = MODS // 512
        tM_base0 = cnt["tM"]
        q_wm = []
        fw_wm = []
        for k in range(KD):
            if k >= 1:
                wait("gpsimd", "tM", tM_base0 + k)
            q_wm.append(dma("gpsimd", wmt[k % 2][:], wmod_sl[128 * k:128 * (k + 1), :], "qG"))
            fw_wm.append(qfence("gpsimd"))
        wait("tensor", "aA", a_siluc)
        t_mod = None
        for k in range(KD):
            wait("tensor", "qG", q_wm[k])
            wfence("tensor", fw_wm[k])
            for n in range(NMOD):
                def mm(t, k=k, n=n):
                    return t.matmul(pb[n][0:1, :], scv[:, k:k + 1],
                                    wmt[k % 2][:, 512 * n:512 * (n + 1)],
                                    start=(k == 0), stop=(k == KD - 1))
                t_mod = op("tensor", mm, "tM") if n == NMOD - 1 else op("tensor", mm)
        wait("vector", "tM", t_mod)
        for n in range(NMOD):
            op("vector", lambda v, n=n: v.tensor_copy(modrow[:, 512 * n:512 * (n + 1)], pb[n][0:1, :]))
        v_mod = fence("vector")
        wait("sync", "vV", v_mod)
        dma("sync", ag1_in[0:1, 0:MODS], modrow[:], "qS")

        ln1_sq(0)
        ln1_stats(0)

        slot_marks = []
        tiles = []
        for w_ap, ntile, width, pbase in ((tsl_g, 2, I, 0), (tsl_u, 2, I, 2), (tsl_d, 8, D, 4)):
            for j in range(ntile):
                nsub = max(1, width // 2048)
                for sub in range(nsub):
                    w0 = 2048 * sub
                    wspan = min(2048, width - w0)
                    tiles.append((w_ap, j, w0, wspan, pbase + j))
        red_seen = {}
        for idx, (w_ap, j, w0, wspan, pslot) in enumerate(tiles):
            if idx >= 2:
                wait("sync", "vV", slot_marks[idx - 2])
            qld = dma("sync", sl_t[idx % 2][:, 0:wspan],
                      w_ap[128 * j:128 * (j + 1), w0:w0 + wspan], "qS")
            wfence("vector", qfence("sync"))
            wait("vector", "qS", qld)
            if pslot not in red_seen:
                slot_marks.append(op("vector", lambda v, idx=idx, wspan=wspan, p=pslot: v.tensor_reduce(
                    partial[:, p:p + 1], sl_t[idx % 2][:, 0:wspan], AX.X, ALU.add,
                    apply_absolute_value=True), "vV"))
            else:
                op("vector", lambda v, idx=idx, wspan=wspan: v.tensor_reduce(
                    partial[:, 15:16], sl_t[idx % 2][:, 0:wspan], AX.X, ALU.add,
                    apply_absolute_value=True))
                op("vector", lambda v: v.drain())
                slot_marks.append(op("vector", lambda v, p=pslot: v.tensor_add(
                    partial[:, p:p + 1], partial[:, p:p + 1], partial[:, 15:16]), "vV"))
                op("vector", lambda v: v.drain())
            red_seen[pslot] = True
        op("vector", lambda v: v.drain())
        v_part = fence("vector")
        ln1_rows(0)
        wait("tensor", "vV", v_part)
        t_psum = op("tensor", lambda t: t.matmul(pb[4][0:1, 0:12], onescol[:],
                                                 partial[:, 0:12], start=True, stop=True), "tM")
        wait("vector", "tM", t_psum)
        op("vector", lambda v: v.tensor_copy(sumrow[:, 0:12], pb[4][0:1, 0:12]))
        v_sum = fence("vector")
        wait("sync", "vV", v_sum)
        q_sum_out = dma("sync", ag1_in[0:1, MODS:MODS + 12], sumrow[:, 0:12], "qS")

        wait("gpsimd", "qS", q_sum_out)
        wfence("gpsimd", qfence("sync"))
        c_ag1 = coll(ag1_in, ag1_out)

        wait("gpsimd", "cC", c_ag1)
        q_un = 0
        for r in range(NC):
            src = ag1_out[r:r + 1, 0:MODS].rearrange("o (q p) -> (o q) p", p=128)
            q_un = dma("gpsimd", modst[12 * r:12 * (r + 1), :], src, "qG")
        for r in range(NC):
            dma("gpsimd", slotrow[0:1, 12 * r:12 * (r + 1)], ag1_out[r:r + 1, MODS:MODS + 12], "qG")
        q_un = dma("gpsimd", sumg[0:NC, 0:3], ag1_out[0:NC, MODS:MODS + 3], "qG")
        fv_un = qfence("gpsimd")
        wait("tensor", "qG", q_un)
        wfence("tensor", fv_un)
        t_un = op("tensor", lambda t: t.transpose(pb[3][:, 0:96], modst[0:96, :], ident[0:96, 0:96]), "tM")
        wait("vector", "tM", t_un)
        op("vector", lambda v: v.tensor_copy(adc[:], pb[3][:, 0:96]))
        op("vector", lambda v: v.drain())
        wait("vector", "qG", q_un)
        wfence("vector", fv_un)
        op("vector", lambda v: v.tensor_scalar(ca256[:], adc[:, KD:2 * KD], 256.0, 256.0, ALU.mult, ALU.add))
        op("vector", lambda v: v.tensor_scalar(sa256[:], adc[:, 0:KD], 256.0, None, ALU.mult))
        op("vector", lambda v: v.tensor_scalar(cb1[:], adc[:, 4 * KD:5 * KD], 1.0, None, ALU.add))
        op("vector", lambda v: v.tensor_copy(sb_c[:], adc[:, 3 * KD:4 * KD]))
        op("vector", lambda v: v.drain())
        for gidx, gdst in ((2, gac), (5, gbc)):
            gsrc = adc[:, gidx * KD:(gidx + 1) * KD]
            op("vector", lambda v, s=gsrc, d=gdst: v.tensor_single_scalar(d[:], s, 0.0, ALU.is_ge))
            op("vector", lambda v: v.drain())
            op("vector", lambda v, d=gdst: v.tensor_scalar(d[:], d[:], 2.0, -1.0, ALU.mult, ALU.add))
            op("vector", lambda v: v.drain())
            op("vector", lambda v, s=gsrc: v.scalar_tensor_tensor(tiny[:], s, -1.0, s, ALU.mult, ALU.max))
            op("vector", lambda v: v.drain())
            op("vector", lambda v: v.tensor_scalar_max(tiny[:], tiny[:], 0.25))
            op("vector", lambda v: v.drain())
            op("vector", lambda v, d=gdst: v.tensor_mul(d[:], d[:], tiny[:]))
            op("vector", lambda v: v.drain())
        v_adaln = fence("vector")
        if DEBUG:
            wait("gpsimd", "vV", v_adaln)
            dma("gpsimd", dbg_adc[:, 0:96], adc[:], "qG")
            dma("gpsimd", dbg_adc[:, 96:96 + KD], gac[:], "qG")
            dma("gpsimd", dbg_adc[:, 96 + KD:96 + 2 * KD], gbc[:], "qG")
        wait("tensor", "qG", q_un)
        wfence("tensor", fv_un)
        for wslot, order in enumerate(FOLD_ORDERS):
            op("vector", lambda v, o=order[0], w=wslot: v.tensor_copy(
                mrow[0:1, 6 + w:7 + w], slotrow[0:1, o:o + 1]))
            op("vector", lambda v: v.drain())
            for o in order[1:]:
                op("vector", lambda v, o=o, w=wslot: v.tensor_add(
                    mrow[0:1, 6 + w:7 + w], mrow[0:1, 6 + w:7 + w], slotrow[0:1, o:o + 1]))
                op("vector", lambda v: v.drain())
        v_fold = fence("vector")
        if DEBUG:
            wait("gpsimd", "vV", v_fold)
            dma("gpsimd", dbg_r2[3:4, 8:104], slotrow[0:1, 0:96], "qG")
            dma("gpsimd", dbg_r2[3:4, 104:116], mrow[0:1, 0:12], "qG")
        op("vector", lambda v: v.tensor_scalar(mrow[:, 3:6], mrow[0:1, 6:9], 1.0 / (D * I), 1e-5,
                                               ALU.mult, ALU.max))
        op("vector", lambda v: v.drain())
        op("vector", lambda v: v.reciprocal(mrow[:, 0:3], mrow[:, 3:6]))
        op("vector", lambda v: v.drain())
        v_mrow = fence("vector")
        wait("tensor", "vV", v_mrow)
        t_swb = op("tensor", lambda t: t.matmul(pb[5][:, 0:8], ones1[:],
                                                mrow[:, 0:8], start=True, stop=True), "tM")
        wait("vector", "tM", t_swb)
        op("vector", lambda v: v.tensor_copy(swv[:], pb[5][:, 0:8]))
        v_swv = fence("vector")

        ln1_chunks(0)

        wait("sync", "vV", ln1["v_ln"])
        wait("sync", "tM", ln1["t_stats"])
        ln1_loads(1)
        ln1_sq(1)
        ln1_stats(1)
        ln1_rows(1)
        ln1_chunks(1)
        v_ln_all = ln1["v_ln"]

        tern_jobs = (("g", tsl_g, tgin, tgout, 2, I, 0), ("u", tsl_u, tuin, tuout, 2, I, 1),
                     ("d", tsl_d, tdin, tdout, 8, D, 2))
        q_t_out = 0
        idx = len(tiles)
        first = True
        ti = 0
        qg_marks = []
        for namew, w_ap, t_in, t_out, ntile, width, scol in tern_jobs:
            for j in range(ntile):
                nsub = max(1, width // 2048)
                for sub in range(nsub):
                    w0 = 2048 * sub
                    wspan = min(2048, width - w0)
                    wait("sync", "vV", slot_marks[idx - 2])
                    qld = dma("sync", sl_t[idx % 2][:, 0:wspan],
                              w_ap[128 * j:128 * (j + 1), w0:w0 + wspan], "qS")
                    wfence("vector", qfence("sync"))
                    wait("vector", "qS", qld)
                    if first:
                        wait("vector", "vV", v_swv)
                        first = False
                    if ti >= 2:
                        wfence("vector", qg_marks[ti - 2])
                    op("vector", lambda v, i=idx, w=wspan, sc=scol: v.tensor_scalar(
                        sl_t[i % 2][:, 0:w], sl_t[i % 2][:, 0:w], swv[:, sc:sc + 1], MAGIC,
                        ALU.mult, ALU.add))
                    op("vector", lambda v, i=idx, w=wspan: v.tensor_scalar(
                        sl_t[i % 2][:, 0:w], sl_t[i % 2][:, 0:w], -MAGIC, -1.0,
                        ALU.add, ALU.max))
                    slot_marks.append(op("vector", lambda v, i=idx, w=wspan: v.tensor_single_scalar(
                        tern_t[i % 2][:, 0:w], sl_t[i % 2][:, 0:w], 1.0, ALU.min), "vV"))
                    wait("gpsimd", "vV", slot_marks[-1])
                    q_t_out = dma("gpsimd", t_in.ap()[128 * j:128 * (j + 1), w0:w0 + wspan],
                                  tern_t[idx % 2][:, 0:wspan], "qG")
                    qg_marks.append(qfence("gpsimd"))
                    if DEBUG and namew == "d":
                        dma("gpsimd", dbg_td[128 * j:128 * (j + 1), w0:w0 + wspan],
                            tern_t[idx % 2][:, 0:wspan], "qG")
                    if DEBUG and namew == "g":
                        dma("gpsimd", dbg_tg[128 * j:128 * (j + 1), w0:w0 + wspan],
                            tern_t[idx % 2][:, 0:wspan], "qG")
                    idx += 1
                    ti += 1
            if namew in ("g", "u"):
                wait("gpsimd", "qG", q_t_out)
                wfence("gpsimd", qfence("gpsimd"))
                coll(t_in.ap(), t_out.ap())
        c_ag2u = cnt["cC"]

        p01.close()
        wait("vector", "qG", cnt["qG"])

        if DEBUG:
            wait("gpsimd", "vV", v_ln_all)
            for k in range(KD):
                dma("gpsimd", dbg_xf[128 * k:128 * (k + 1), :], xf16[:, TC * k:TC * (k + 1)], "qG")
        pln.close()
        barrier_dma()

        psw = ExitStack()
        ES = psw.enter_context
        wsl = [ES(nc.sbuf_tensor(f"wsl{i}", [128, 3 * KD * 512], F16)) for i in range(2)]
        f_m = ES(nc.sbuf_tensor("f_m", [128, TC], F32))
        i_m = ES(nc.sbuf_tensor("i_m", [128, TC], F32))
        t_m = ES(nc.sbuf_tensor("t_m", [128, TC], F32))
        h_m = ES(nc.sbuf_tensor("h_m", [128, TC], F32))
        P_m = ES(nc.sbuf_tensor("P_m", [128, TC], F32))
        g16m = ES(nc.sbuf_tensor("g16m", [128, TC], F16))

        w_aps = (wi_d, wf_d, wg_d)
        NG = 4
        t_grp = []
        q_grp = []
        f_grp = []
        a_evac = []
        v_scan = []
        q_spill = []
        f_spill = []

        def wslice(buf, w, k):
            base = (w * KD + k) * 512
            return buf[:, base:base + 512]

        for grp in range(NG + 1):
            if grp < NG:
                if grp >= 2:
                    wait("gpsimd", "tM", t_grp[grp - 2])
                qv = 0
                for w in range(3):
                    for k in range(KD):
                        qv = dma("gpsimd", wslice(wsl[grp % 2], w, k),
                                 w_aps[w][128 * k:128 * (k + 1), 512 * grp:512 * (grp + 1)], "qG")
                q_grp.append(qv)
                f_grp.append(qfence("gpsimd"))
            if grp == 0:
                continue
            g = grp - 1
            wait("tensor", "qG", q_grp[g])
            wfence("tensor", f_grp[g])
            if g == 0:
                wait("tensor", "vV", v_ln_all)
            for mloc in range(4):
                m = 4 * g + mloc
                if m > 0:
                    wait("tensor", "aA", a_evac[m - 1])
                lastmm = None
                wmark = {}
                for w in range(3):
                    for n in range(NT):
                        for k in range(KD):
                            def mm(t, w=w, n=n, k=k, mloc=mloc, g=g):
                                return t.matmul(
                                    pb[2 * w + n][:],
                                    wslice(wsl[g % 2], w, k)[:, 128 * mloc:128 * (mloc + 1)],
                                    xf16[:, TC * k + TT * n:TC * k + TT * (n + 1)],
                                    start=(k == 0), stop=(k == KD - 1))
                            if n == NT - 1 and k == KD - 1:
                                lastmm = op("tensor", mm, "tM")
                                wmark[w] = lastmm
                            else:
                                op("tensor", mm)
                if mloc == 3:
                    t_grp.append(lastmm)
                wait("scalar", "tM", wmark[0])
                if m > 0:
                    wait("scalar", "vV", v_scan[m - 1])
                    wfence("scalar", f_spill[m - 1])
                for n in range(NT):
                    op("scalar", lambda s, m=m, n=n: s.activation(
                        i_m[:, TT * n:TT * (n + 1)], pb[0 + n][:], AF.Copy,
                        scale=sivec[:, m:m + 1]))
                wait("scalar", "tM", wmark[1])
                for n in range(NT):
                    op("scalar", lambda s, m=m, n=n: s.activation(
                        f_m[:, TT * n:TT * (n + 1)], pb[2 + n][:], AF.Sigmoid,
                        scale=sfvec[:, m:m + 1]))
                wait("scalar", "tM", wmark[2])
                a_last = None
                for n in range(NT):
                    a_last = op("scalar", lambda s, m=m, n=n: s.activation(
                        g16m[:, TT * n:TT * (n + 1)], pb[4 + n][:], AF.Copy,
                        scale=sgvec[:, m:m + 1]), "aA")
                a_evac.append(a_last)
                wait("gpsimd", "aA", a_last)
                dma("gpsimd", g_sp[128 * m:128 * (m + 1), :], g16m[:], "qG")
                wait("vector", "aA", a_last)
                if m > 0:
                    wfence("vector", f_spill[m - 1])
                op("vector", lambda v: v.tensor_scalar(t_m[:], f_m[:], -1.0, 1.0, ALU.mult, ALU.add))
                op("vector", lambda v: v.tensor_mul(i_m[:], t_m[:], i_m[:]))
                op("vector", lambda v: v.tensor_tensor_scan(
                    P_m[:, 0:TT], f_m[:, 0:TT], zeros[:], 1.0, ALU.mult, ALU.add))
                op("vector", lambda v: v.drain())
                op("vector", lambda v: v.tensor_tensor_scan(
                    P_m[:, TT:TC], f_m[:, TT:TC], zeros[:], P_m[:, TT - 1:TT], ALU.mult, ALU.add))
                op("vector", lambda v: v.tensor_tensor_scan(
                    h_m[:, 0:TT], f_m[:, 0:TT], i_m[:, 0:TT], 0.0, ALU.mult, ALU.add))
                op("vector", lambda v: v.drain())
                op("vector", lambda v: v.tensor_tensor_scan(
                    h_m[:, TT:TC], f_m[:, TT:TC], i_m[:, TT:TC], h_m[:, TT - 1:TT],
                    ALU.mult, ALU.add))
                v_scan.append(fence("vector"))
                wait("gpsimd", "vV", v_scan[m])
                if DEBUG and m == 0:
                    dma("gpsimd", dbg_fi[0:128, :], f_m[:], "qG")
                    dma("gpsimd", dbg_fi[128:256, :], i_m[:], "qG")
                dma("gpsimd", h_sp[128 * m:128 * (m + 1), :], h_m[:], "qG")
                dma("gpsimd", p_sp[128 * m:128 * (m + 1), :], P_m[:], "qG")
                dma("gpsimd", sum_in[m:m + 1, :], P_m[:, TC - 1:TC], "qG")
                q_spill.append(dma("gpsimd", sum_in[KD + m:KD + m + 1, :], h_m[:, TC - 1:TC], "qG"))
                f_spill.append(qfence("gpsimd"))
        q_p1_spill = cnt["qG"]
        wait("gpsimd", "qG", q_p1_spill)
        fv3 = qfence("gpsimd")
        wfence("gpsimd", fv3)

        c_ag3 = coll(sum_in, sum_out)
        coll(tdin.ap(), tdout.ap())
        c_ag2d = cnt["cC"]

        wait("gpsimd", "cC", c_ag3)
        Pj = ES(nc.sbuf_tensor("Pj", [128, KD], F32))
        Hj = ES(nc.sbuf_tensor("Hj", [128, KD], F32))
        PHj = ES(nc.sbuf_tensor("PHj", [128, 2 * KD * NC], F32))
        carst = ES(nc.sbuf_tensor("carst", [128, 128], F32))
        carst2 = ES(nc.sbuf_tensor("carst2", [128, 128], F32))
        sAb = ES(nc.sbuf_tensor("sAb", [128, KD], F32))
        sBb = ES(nc.sbuf_tensor("sBb", [128, KD], F32))
        op("vector", lambda v: v.memset(sAb[:], 0.0))
        cur, nxt = sAb, sBb
        qh_ = 0
        dma("gpsimd", carst[:], sum_out[0:128, :], "qG")
        qh_ = dma("gpsimd", carst2[:], sum_out[128:256, :], "qG")
        fvj = qfence("gpsimd")
        wait("tensor", "qG", qh_)
        wfence("tensor", fvj)
        t_c1 = op("tensor", lambda t: t.transpose(pb[6][:, 0:128], carst[:], ident[:]), "tM")
        t_c2 = op("tensor", lambda t: t.transpose(pb[7][:, 0:128], carst2[:], ident[:]), "tM")
        wait("vector", "tM", t_c2)
        op("vector", lambda v: v.tensor_copy(PHj[:, 0:128], pb[6][:, 0:128]))
        op("vector", lambda v: v.tensor_copy(PHj[:, 128:256], pb[7][:, 0:128]))
        op("vector", lambda v: v.drain())
        for j in range(NC):
            Pj_s = PHj[:, 32 * j:32 * j + KD]
            Hj_s = PHj[:, 32 * j + KD:32 * j + 2 * KD]
            op("vector", lambda v, j=j, P=Pj_s: v.tensor_scalar(
                Pj[:], P, cm0[:, j:j + 1], cm1[:, j:j + 1], ALU.mult, ALU.add))
            op("vector", lambda v, j=j, H=Hj_s: v.tensor_scalar(
                Hj[:], H, cm0[:, j:j + 1], None, ALU.mult))
            op("vector", lambda v: v.drain())
            op("vector", lambda v, c=cur: v.tensor_mul(tiny[:], Pj[:], c[:]))
            op("vector", lambda v: v.drain())
            op("vector", lambda v, nx=nxt: v.tensor_add(nx[:], tiny[:], Hj[:]))
            op("vector", lambda v: v.drain())
            cur, nxt = nxt, cur
        op("vector", lambda v: v.drain())
        op("vector", lambda v, c=cur: v.tensor_copy(carry[:], c[:]))
        op("vector", lambda v: v.drain())
        v_carry = fence("vector")
        if DEBUG:
            wait("gpsimd", "vV", v_carry)
            dma("gpsimd", dbg_carry, carry[:], "qG")
        psw.close()
        pxf.close()
        barrier_dma()

        po = ExitStack()
        o16 = po.enter_context(nc.sbuf_tensor("o16", [128, KD * TC], F16))
        p2a = ExitStack()
        E2 = p2a.enter_context
        h16s = E2(nc.sbuf_tensor("h16s", [128, KD * TT], F16))
        P16s = E2(nc.sbuf_tensor("P16s", [128, KD * TT], F16))
        g16s = E2(nc.sbuf_tensor("g16s", [128, KD * TT], F16))
        hw = E2(nc.sbuf_tensor("hw", [128, KD * TT], F32))
        hsq16 = E2(nc.sbuf_tensor("hsq16", [128, KD * TT], F16))
        sg16 = E2(nc.sbuf_tensor("sg16", [128, KD * TT], F16))
        rmsr = E2(nc.sbuf_tensor("rmsr", [128, TT], F32))
        vt2 = E2(nc.sbuf_tensor("vt2", [128, TT], F32))

        wait("sync", "qG", q_p1_spill)
        v_o16 = None
        a_rms_prev = None
        t_rms_prev = None
        for n in range(NT):
            if n > 0:
                wait("sync", "vV", v_o16)
                wait("sync", "tM", t_rms_prev)
            q_h = 0
            for k in range(KD):
                dma("sync", h16s[:, TT * k:TT * (k + 1)],
                    h_sp[128 * k:128 * (k + 1), TT * n:TT * (n + 1)], "qS")
                dma("sync", P16s[:, TT * k:TT * (k + 1)],
                    p_sp[128 * k:128 * (k + 1), TT * n:TT * (n + 1)], "qS")
                q_h = dma("sync", g16s[:, TT * k:TT * (k + 1)],
                          g_sp[128 * k:128 * (k + 1), TT * n:TT * (n + 1)], "qS")
            f_h = qfence("sync")
            wait("vector", "qS", q_h)
            wfence("vector", f_h)
            if n == 0:
                wait("vector", "vV", v_carry)
            for k in range(KD):
                op("vector", lambda v, k=k: v.scalar_tensor_tensor(
                    hw[:, TT * k:TT * (k + 1)], P16s[:, TT * k:TT * (k + 1)],
                    carry[:, k:k + 1], h16s[:, TT * k:TT * (k + 1)], ALU.mult, ALU.add))
            v_fix = fence("vector")
            wait("scalar", "vV", v_fix)
            a_hsq = op("scalar", lambda s: s.activation(hsq16[:], hw[:], AF.Square), "aA")
            wait("tensor", "aA", a_hsq)
            if n > 0:
                wait("tensor", "aA", a_rms_prev)
            t_rms = None
            for k in range(KD):
                def mm(t, k=k):
                    return t.matmul(pb[6][:], ones16[:], hsq16[:, TT * k:TT * (k + 1)],
                                    start=(k == 0), stop=(k == KD - 1))
                t_rms = op("tensor", mm, "tM") if k == KD - 1 else op("tensor", mm)
            t_rms_prev = t_rms
            wait("scalar", "tM", t_rms)
            op("scalar", lambda s: s.activation(vt2[:], pb[6][:], AF.Ln, scale=1.0 / D, bias=epsc[:, 0:1]))
            a_rms = op("scalar", lambda s: s.activation(rmsr[:], vt2[:], AF.Exp, scale=-0.5), "aA")
            a_rms_prev = a_rms
            a_sg = op("scalar", lambda s: s.activation(sg16[:], g16s[:], AF.Silu), "aA")
            wait("vector", "aA", a_sg)
            for k in range(KD):
                sl = slice(TT * k, TT * (k + 1))
                od = o16[:, TC * k + TT * n: TC * k + TT * (n + 1)]
                op("vector", lambda v, sl=sl: v.tensor_mul(hw[:, sl], hw[:, sl], rmsr[:]))
                op("vector", lambda v, sl=sl: v.tensor_mul(hw[:, sl], hw[:, sl], sg16[:, sl]))
                op("vector", lambda v, sl=sl, k=k, od=od: v.tensor_scalar(
                    od, hw[:, sl], gnwso[:, k:k + 1], None, ALU.mult))
            v_o16 = fence("vector")
        if DEBUG:
            wait("gpsimd", "vV", v_o16)
            for k in range(KD):
                dma("gpsimd", dbg_o16[128 * k:128 * (k + 1), :], o16[:, TC * k:TC * (k + 1)], "qG")
        p2a.close()
        barrier_dma()

        p2b = ExitStack()
        E2 = p2b.enter_context
        xres = E2(nc.sbuf_tensor("xres", [128, KD * TC], F32))
        wosl = [E2(nc.sbuf_tensor(f"wosl{i}", [128, KD * 512], F16)) for i in range(2)]
        x1t = E2(nc.sbuf_tensor("x1t", [128, TT], F32))

        q_xr = 0
        for k in range(KD):
            q_xr = dma("sync", xres[:, TC * k:TC * (k + 1)], x_in[128 * k:128 * (k + 1), :], "qS")
        f_xr = qfence("sync")

        def woslice(buf, k):
            return buf[:, k * 512:(k + 1) * 512]

        t_wog = []
        q_wog = []
        f_wog = []
        v_ev2 = []
        q_x1s = 0
        f_x1t = None
        for grp in range(4 + 1):
            if grp < 4:
                if grp >= 2:
                    wait("gpsimd", "tM", t_wog[grp - 2])
                qv = 0
                for k in range(KD):
                    qv = dma("gpsimd", woslice(wosl[grp % 2], k),
                             woT[128 * k:128 * (k + 1), 512 * grp:512 * (grp + 1)], "qG")
                q_wog.append(qv)
                f_wog.append(qfence("gpsimd"))
            if grp == 0:
                continue
            g = grp - 1
            wait("tensor", "qG", q_wog[g])
            wfence("tensor", f_wog[g])
            if g == 0:
                wait("tensor", "vV", v_o16)
            for mloc in range(4):
                m = 4 * g + mloc
                pair = m % 3
                if m >= 3:
                    wait("tensor", "vV", v_ev2[m - 3])
                lastmm = None
                for k in range(KD):
                    for n in range(NT):
                        def mm(t, k=k, n=n, mloc=mloc, g=g, pair=pair):
                            return t.matmul(pb[2 * pair + n][:],
                                            woslice(wosl[g % 2], k)[:, 128 * mloc:128 * (mloc + 1)],
                                            o16[:, TC * k + TT * n:TC * k + TT * (n + 1)],
                                            start=(k == 0), stop=(k == KD - 1))
                        if k == KD - 1 and n == NT - 1:
                            lastmm = op("tensor", mm, "tM")
                        else:
                            op("tensor", mm)
                if mloc == 3:
                    t_wog.append(lastmm)
                wait("vector", "tM", lastmm)
                if m == 0:
                    wait("vector", "qS", q_xr)
                    wfence("vector", f_xr)
                vlast = None
                for n in range(NT):
                    ps = pb[2 * pair + n]
                    if f_x1t is not None:
                        wfence("vector", f_x1t)
                    op("vector", lambda v, ps=ps: v.tensor_scalar(
                        x1t[:], ps[:], 256.0, MAGIC, ALU.mult, ALU.add))
                    op("vector", lambda v: v.tensor_scalar(
                        x1t[:], x1t[:], -MAGIC, 1.0 / 256.0, ALU.add, ALU.mult))
                    if DEBUG:
                        vdbg = fence("vector")
                        wait("gpsimd", "vV", vdbg)
                        qdbg = dma("gpsimd", dbg_att[128 * m:128 * (m + 1), TT * n:TT * (n + 1)], x1t[:], "qG")
                        wait("vector", "qG", qdbg)
                    vlast = op("vector", lambda v, m=m, n=n: v.scalar_tensor_tensor(
                        x1t[:], x1t[:], gac[:, m:m + 1],
                        xres[:, TC * m + TT * n:TC * m + TT * (n + 1)], ALU.mult, ALU.add), "vV")
                    wait("gpsimd", "vV", vlast)
                    q_x1s = dma("gpsimd", x1_sp[128 * m:128 * (m + 1), TT * n:TT * (n + 1)], x1t[:], "qG")
                    f_x1t = qfence("gpsimd")
                v_ev2.append(vlast)
        q_x1_spill = cnt["qG"]
        p2b.close()
        po.close()
        barrier_dma()

        pqx = ExitStack()
        qx = pqx.enter_context(nc.sbuf_tensor("qx", [128, KD * TC], BF16))
        p4a = ExitStack()
        E4 = p4a.enter_context
        x1f = E4(nc.sbuf_tensor("x1f", [128, KD * TT], F32))
        x1sq = E4(nc.sbuf_tensor("x1sq", [128, KD * TT], F32))
        h2f = E4(nc.sbuf_tensor("h2f", [128, KD * TT], F32))
        mrep2 = E4(nc.sbuf_tensor("mrep2", [128, TT], F32))
        rstd2 = E4(nc.sbuf_tensor("rstd2", [128, TT], F32))
        vtmp2 = E4(nc.sbuf_tensor("vtmp2", [128, TT], F32))
        ttmp2 = E4(nc.sbuf_tensor("ttmp2", [128, TT], F32))
        mx = E4(nc.sbuf_tensor("mx", [128, TT], F32))
        s1b = E4(nc.sbuf_tensor("s1b", [128, TT], F32))

        f_x1sp = qfence("gpsimd")
        wait("sync", "qG", q_x1_spill)
        wfence("sync", f_x1sp)
        v_qx = None
        t_ln2_prev = None
        a_sq2 = None
        v_var2 = None
        for n in range(NT):
            if n > 0:
                wait("sync", "vV", v_qx)
                wait("sync", "tM", t_ln2_prev)
            q_x = 0
            for k in range(KD):
                q_x = dma("sync", x1f[:, TT * k:TT * (k + 1)],
                          x1_sp[128 * k:128 * (k + 1), TT * n:TT * (n + 1)], "qS")
            f_x2 = qfence("sync")
            wait("scalar", "qS", q_x)
            wfence("scalar", f_x2)
            a_sq2 = op("scalar", lambda s: s.activation(x1sq[:], x1f[:], AF.Square), "aA")
            wait("tensor", "qS", q_x)
            wfence("tensor", f_x2)
            if n > 0:
                wait("tensor", "vV", v_var2)
            for k in range(KD):
                op("tensor", lambda t, k=k: t.matmul(
                    pb[6][:], ones32[:], x1f[:, TT * k:TT * (k + 1)],
                    start=(k == 0), stop=(k == KD - 1)))
            wait("tensor", "aA", a_sq2)
            t_ln2 = None
            for k in range(KD):
                def mm(t, k=k):
                    return t.matmul(pb[7][:], ones32[:], x1sq[:, TT * k:TT * (k + 1)],
                                    start=(k == 0), stop=(k == KD - 1))
                t_ln2 = op("tensor", mm, "tM") if k == KD - 1 else op("tensor", mm)
            t_ln2_prev = t_ln2
            wait("vector", "tM", t_ln2)
            op("vector", lambda v: v.tensor_scalar(mrep2[:], pb[6][:], 1.0 / D, None, ALU.mult))
            op("vector", lambda v: v.tensor_scalar(vtmp2[:], pb[7][:], 1.0 / D, None, ALU.mult))
            op("vector", lambda v: v.scalar_tensor_tensor(ttmp2[:], mrep2[:], -1.0, mrep2[:],
                                                          ALU.mult, ALU.mult))
            op("vector", lambda v: v.tensor_add(vtmp2[:], vtmp2[:], ttmp2[:]))
            v_var2 = fence("vector")
            wait("scalar", "vV", v_var2)
            op("scalar", lambda s: s.activation(ttmp2[:], vtmp2[:], AF.Ln, bias=epsc[:, 0:1]))
            a_rstd2 = op("scalar", lambda s: s.activation(rstd2[:], ttmp2[:], AF.Exp, scale=-0.5), "aA")
            wait("vector", "aA", a_rstd2)
            wait("vector", "qS", q_x)
            wfence("vector", f_x2)
            op("vector", lambda v: v.memset(mx[:], 0.0))
            for k in range(KD):
                sl = slice(TT * k, TT * (k + 1))
                op("vector", lambda v, sl=sl: v.tensor_sub(ttmp2[:], x1f[:, sl], mrep2[:]))
                op("vector", lambda v: v.tensor_mul(ttmp2[:], ttmp2[:], rstd2[:]))
                op("vector", lambda v, k=k, sl=sl: v.tensor_scalar(
                    h2f[:, sl], ttmp2[:], cb1[:, k:k + 1], sb_c[:, k:k + 1], ALU.mult, ALU.add))
                op("vector", lambda v, sl=sl: v.scalar_tensor_tensor(
                    ttmp2[:], h2f[:, sl], -1.0, h2f[:, sl], ALU.mult, ALU.max))
                op("vector", lambda v: v.tensor_max(mx[:], mx[:], ttmp2[:]))
            v_mx = fence("vector")
            part_max_row(mx, TT, 6, rowA, v_mx)
            op("vector", lambda v: v.tensor_scalar_max(rowB[0:1, 0:TT], rowA[0:1, 0:TT], 1e-5))
            op("vector", lambda v: v.drain())
            op("vector", lambda v, n=n: v.tensor_scalar(
                dqrow[0:1, TT * n:TT * (n + 1)], rowB[0:1, 0:TT], 1.0 / 127.0, None, ALU.mult))
            op("vector", lambda v: v.reciprocal(rowA[0:1, 0:TT], rowB[0:1, 0:TT]))
            op("vector", lambda v: v.drain())
            op("vector", lambda v: v.tensor_scalar(rowB[0:1, 0:TT], rowA[0:1, 0:TT], 127.0,
                                                   None, ALU.mult))
            op("vector", lambda v: v.drain())
            v_s1row = fence("vector")
            wait("tensor", "vV", v_s1row)
            t_s1b = op("tensor", lambda t: t.matmul(pb[6][:], ones1[:], rowB[0:1, 0:TT],
                                                    start=True, stop=True), "tM")
            wait("vector", "tM", t_s1b)
            op("vector", lambda v: v.tensor_copy(s1b[:], pb[6][:]))
            for k in range(KD):
                sl = slice(TT * k, TT * (k + 1))
                qd = qx[:, TC * k + TT * n:TC * k + TT * (n + 1)]
                op("vector", lambda v, sl=sl: v.tensor_mul(ttmp2[:], h2f[:, sl], s1b[:]))
                op("vector", lambda v, qd=qd: v.tensor_scalar(
                    qd, ttmp2[:], MAGIC, -MAGIC, ALU.add, ALU.add))
            v_qx = fence("vector")
        wait("tensor", "vV", v_qx)
        t_dq1 = op("tensor", lambda t: t.matmul(pb[6][:], ones1[:], dqrow[0:1, 0:TT],
                                                start=True, stop=True), "tM")
        t_dq2 = op("tensor", lambda t: t.matmul(pb[7][:], ones1[:], dqrow[0:1, TT:TC],
                                                start=True, stop=True), "tM")
        wait("vector", "tM", t_dq2)
        op("vector", lambda v: v.tensor_copy(dqb[:, 0:TT], pb[6][:]))
        op("vector", lambda v: v.tensor_copy(dqb[:, TT:TC], pb[7][:]))
        v_dqb = fence("vector")
        if DEBUG:
            wait("gpsimd", "vV", v_dqb)
            for k in range(KD):
                dma("gpsimd", dbg_qx[128 * k:128 * (k + 1), :], qx[:, TC * k:TC * (k + 1)], "qG")
            dma("gpsimd", dbg_rows[0:1, :], dqrow[:], "qG")
        p4a.close()
        barrier_dma()

        p4b = ExitStack()
        E4 = p4b.enter_context
        wt = [E4(nc.sbuf_tensor(f"wt4{i}", [128, 2 * KD * 512], BF16)) for i in range(2)]
        gu32 = E4(nc.sbuf_tensor("gu32", [128, TC], F32))
        silu16 = E4(nc.sbuf_tensor("silu16", [128, TC], F16))
        up32 = E4(nc.sbuf_tensor("up32", [128, TC], F32))
        h16r = [E4(nc.sbuf_tensor(f"h16r{i}", [128, TC], F16)) for i in range(2)]
        hq16r = [E4(nc.sbuf_tensor(f"hq16r{i}", [128, TC], F16)) for i in range(2)]
        mx2a = E4(nc.sbuf_tensor("mx2a", [128, TC], F32))
        mx2b = E4(nc.sbuf_tensor("mx2b", [128, TC], F32))
        op("vector", lambda v: v.memset(mx2a[:], 0.0))

        NSEC = 16

        def wt_slice(buf, w, k):
            base = (w * KD + k) * 512
            return buf[:, base:base + 512]

        wait("sync", "cC", c_ag2u)
        t_sec = []
        q_sec = []
        f_sec = []
        ev_fence = []
        a_silu_prev = None
        a_hsq_prev = None
        t_statsm_prev = None
        pend_stats = []
        stats_done = {}
        v_mx2_prev = None
        q_hsp_prev = 0
        pair_step = 0
        t_stats_last = None
        mx_cur, mx_nxt = mx2a, mx2b
        for sec in range(NSEC + 1):
            if sec < NSEC:
                if sec >= 2:
                    wait("sync", "tM", t_sec[sec - 2])
                qv = 0
                for w, wdram in enumerate((tgout, tuout)):
                    for k in range(KD):
                        qv = dma("sync", wt_slice(wt[sec % 2], w, k),
                                 wdram.ap()[128 * k:128 * (k + 1), 512 * sec:512 * (sec + 1)], "qS")
                q_sec.append(qv)
                f_sec.append(qfence("sync"))
            if sec == 0:
                continue
            s_ = sec - 1
            wait("tensor", "qS", q_sec[s_])
            wfence("tensor", f_sec[s_])
            if s_ == 0:
                wait("tensor", "vV", v_dqb)
            for mloc in range(4):
                m = 4 * s_ + mloc
                gate_pair = pair_step % 3
                if pair_step >= 3:
                    wait("tensor", "vV", ev_fence[pair_step - 3])
                if len(pend_stats) >= 2:
                    pm, phq, pahsq = pend_stats.pop(0)
                    wait("tensor", "aA", pahsq)
                    for n in range(NT):
                        def mm(t, n=n, pm=pm, phq=phq):
                            return t.matmul(pb[6 + n][:], ones16[:],
                                            phq[:, TT * n:TT * (n + 1)],
                                            start=(pm == 0), stop=(pm == KI - 1))
                        t_statsm_prev = op("tensor", mm, "tM") if n == NT - 1 else op("tensor", mm)
                    stats_done[pm] = t_statsm_prev
                lastmm = None
                for k in range(KD):
                    for n in range(NT):
                        def mm(t, k=k, n=n, mloc=mloc, s_=s_, pair=gate_pair):
                            return t.matmul(
                                pb[2 * pair + n][:],
                                wt_slice(wt[s_ % 2], 0, k)[:, 128 * mloc:128 * (mloc + 1)],
                                qx[:, TC * k + TT * n:TC * k + TT * (n + 1)],
                                start=(k == 0), stop=(k == KD - 1))
                        if k == KD - 1 and n == NT - 1:
                            lastmm = op("tensor", mm, "tM")
                        else:
                            op("tensor", mm)
                t_gate = lastmm
                pair_step += 1
                up_pair = pair_step % 3
                if pair_step >= 3:
                    wait("tensor", "vV", ev_fence[pair_step - 3])
                for k in range(KD):
                    for n in range(NT):
                        def mm(t, k=k, n=n, mloc=mloc, s_=s_, pair=up_pair):
                            return t.matmul(
                                pb[2 * pair + n][:],
                                wt_slice(wt[s_ % 2], 1, k)[:, 128 * mloc:128 * (mloc + 1)],
                                qx[:, TC * k + TT * n:TC * k + TT * (n + 1)],
                                start=(k == 0), stop=(k == KD - 1))
                        if k == KD - 1 and n == NT - 1:
                            lastmm = op("tensor", mm, "tM")
                        else:
                            op("tensor", mm)
                t_up = lastmm
                pair_step += 1
                wait("vector", "tM", t_gate)
                wait("vector", "aA", a_silu_prev)
                for n in range(NT):
                    op("vector", lambda v, n=n, p=gate_pair: v.scalar_tensor_tensor(
                        gu32[:, TT * n:TT * (n + 1)], pb[2 * p + n][:], swv[:, 3:4],
                        dqb[:, TT * n:TT * (n + 1)], ALU.mult, ALU.mult))
                ev_fence.append(fence("vector"))
                wait("scalar", "vV", ev_fence[-1])
                if m >= 2 and (m - 2) in stats_done:
                    wait("scalar", "tM", stats_done[m - 2])
                wait("scalar", "vV", v_mx2_prev)
                a_silu = op("scalar", lambda s: s.activation(silu16[:], gu32[:], AF.Silu), "aA")
                a_silu_prev = a_silu
                wait("vector", "tM", t_up)
                wait("vector", "aA", a_hsq_prev)
                if q_hsp_prev and m % 2 == 0:
                    wait("vector", "qG", q_hsp_prev)
                    wfence("vector", qfence("gpsimd"))
                h16 = h16r[m % 2]
                hq16 = hq16r[m % 2]
                for n in range(NT):
                    op("vector", lambda v, n=n, p=up_pair: v.scalar_tensor_tensor(
                        up32[:, TT * n:TT * (n + 1)], pb[2 * p + n][:], swv[:, 4:5],
                        dqb[:, TT * n:TT * (n + 1)], ALU.mult, ALU.mult))
                wait("vector", "aA", a_silu)
                op("vector", lambda v, h16=h16: v.tensor_mul(h16[:], silu16[:], up32[:]))
                v_h16 = fence("vector")
                ev_fence.append(v_h16)
                wait("scalar", "vV", v_h16)
                a_hsq = op("scalar", lambda s, h16=h16, hq16=hq16: s.activation(hq16[:], h16[:], AF.Square), "aA")
                a_hsq_prev = a_hsq
                pend_stats.append((m, hq16, a_hsq))
                wait("vector", "aA", a_hsq)
                v_mx2_prev = op("vector", lambda v, m=m, a=mx_cur, b=mx_nxt, hq16=hq16: v.scalar_tensor_tensor(
                    b[:], hq16[:], lnw2[:, m:m + 1], a[:], ALU.mult, ALU.max), "vV")
                mx_cur, mx_nxt = mx_nxt, mx_cur
                wait("gpsimd", "vV", v_mx2_prev)
                q_hsp_prev = dma("gpsimd", hf_sp[128 * m:128 * (m + 1), :], h16[:], "qG")
            if mloc == 3:
                t_sec.append(t_up)
        while pend_stats:
            pm, phq, pahsq = pend_stats.pop(0)
            wait("tensor", "aA", pahsq)
            for n in range(NT):
                def mm(t, n=n, pm=pm, phq=phq):
                    return t.matmul(pb[6 + n][:], ones16[:],
                                    phq[:, TT * n:TT * (n + 1)],
                                    start=(pm == 0), stop=(pm == KI - 1))
                t_statsm_prev = op("tensor", mm, "tM") if n == NT - 1 else op("tensor", mm)
            stats_done[pm] = t_statsm_prev
        t_stats_last = t_statsm_prev
        q_hf_spill = cnt["qG"]
        f_hfsp = qfence("gpsimd")
        mx2_final = mx_cur

        rbrow = E4(nc.sbuf_tensor("rbrow", [1, TC], F32))
        wait("scalar", "tM", t_stats_last)
        for n in range(NT):
            op("scalar", lambda s, n=n: s.activation(rowA[0:1, TT * n:TT * (n + 1)],
                                                     pb[6 + n][0:1, :], AF.Ln, scale=1.0 / I, bias=epsc[0:1, 1:2]))
        a_rb = op("scalar", lambda s: s.activation(rbrow[0:1, :], rowA[0:1, :], AF.Exp, scale=-0.5), "aA")
        v_mx2_mark = v_mx2_prev
        part_max_row(mx2_final, TC, 5, rowB, v_mx2_mark)
        v_pm = fence("vector")
        wait("scalar", "vV", v_pm)
        a_m2 = op("scalar", lambda s: s.activation(rowB[0:1, :], rowB[0:1, :], AF.Sqrt), "aA")
        wait("vector", "aA", max(a_rb, a_m2))
        op("vector", lambda v: v.tensor_mul(rowB[0:1, :], rowB[0:1, :], rbrow[0:1, :]))
        op("vector", lambda v: v.tensor_scalar_max(rowB[0:1, :], rowB[0:1, :], 1e-5))
        op("vector", lambda v: v.drain())
        op("vector", lambda v: v.tensor_scalar(dq2row[0:1, :], rowB[0:1, :], 1.0 / 127.0, None, ALU.mult))
        op("vector", lambda v: v.reciprocal(rowA[0:1, :], rowB[0:1, :]))
        op("vector", lambda v: v.drain())
        op("vector", lambda v: v.tensor_scalar(rowB[0:1, :], rowA[0:1, :], 127.0, None, ALU.mult))
        op("vector", lambda v: v.drain())
        op("vector", lambda v: v.tensor_mul(c2row[0:1, :], rowB[0:1, :], rbrow[0:1, :]))
        op("vector", lambda v: v.drain())
        v_rows = fence("vector")
        if DEBUG:
            wait("gpsimd", "vV", v_rows)
            dma("gpsimd", dbg_r2[3:4, 0:8], swv[0:1, 0:8], "qG")
            dma("gpsimd", dbg_r2[0:1, :], c2row[:], "qG")
            dma("gpsimd", dbg_r2[1:2, :], dq2row[:], "qG")
            dma("gpsimd", dbg_r2[2:3, :], rbrow[:], "qG")
        wait("tensor", "vV", v_rows)
        for bi, (rsrc, bdst) in enumerate(((c2row, c2b), (dq2row, dq2b))):
            for j in range(NT):
                t_b = op("tensor", lambda t, r=rsrc, j=j, bi=bi: t.matmul(
                    pb[2 * bi + j][:], ones1[:], r[0:1, TT * j:TT * (j + 1)],
                    start=True, stop=True), "tM")
                wait("vector", "tM", t_b)
                op("vector", lambda v, b=bdst, j=j, bi=bi: v.tensor_copy(
                    b[:, TT * j:TT * (j + 1)], pb[2 * bi + j][:]))
        v_bc2 = fence("vector")
        p4b.close()
        pqx.close()
        barrier_dma()

        p5 = ExitStack()
        E5 = p5.enter_context
        qh = E5(nc.sbuf_tensor("qh", [128, KI * TT], BF16))
        h16b = [E5(nc.sbuf_tensor(f"h16b{i}", [128, 8 * TT], F16)) for i in range(2)]
        qtmp = E5(nc.sbuf_tensor("qtmp", [128, TT], F32))
        wd_t = [E5(nc.sbuf_tensor(f"wd{i}", [128, 8 * 1024], BF16)) for i in range(2)]
        x1c = E5(nc.sbuf_tensor("x1c", [128, 8 * TT], F32))
        ffc = [E5(nc.sbuf_tensor(f"ffc{i}", [128, TT], F32)) for i in range(4)]

        wait("sync", "cC", c_ag2d)
        wait("sync", "qG", q_hf_spill)
        wfence("sync", f_hfsp)
        t_dn_last = None
        v_outev = None
        out_marks = []
        estep = 0
        for n in range(NT):
            v_qh_marks = []
            NB = 8
            for b in range(KI // NB):
                if b >= 2:
                    wait("sync", "vV", v_qh_marks[(b - 2) * NB + NB - 1])
                qld = 0
                for kk in range(NB):
                    k = b * NB + kk
                    qld = dma("sync", h16b[b % 2][:, TT * kk:TT * (kk + 1)],
                              hf_sp[128 * k:128 * (k + 1), TT * n:TT * (n + 1)], "qS")
                f_hb = qfence("sync")
                wait("vector", "qS", qld)
                wfence("vector", f_hb)
                if b == 0 and n == 0:
                    wait("vector", "vV", v_bc2)
                if b == 0 and n == 1:
                    wait("vector", "tM", t_dn_last)
                for kk in range(NB):
                    k = b * NB + kk
                    op("vector", lambda v, k=k, kk=kk, b=b, n=n: v.scalar_tensor_tensor(
                        qtmp[:], h16b[b % 2][:, TT * kk:TT * (kk + 1)], lnwv[:, k:k + 1],
                        c2b[:, TT * n:TT * (n + 1)], ALU.mult, ALU.mult))
                    v_qh_marks.append(op("vector", lambda v, k=k: v.tensor_scalar(
                        qh[:, TT * k:TT * (k + 1)], qtmp[:], MAGIC, -MAGIC, ALU.add, ALU.add), "vV"))
            v_qh = v_qh_marks[-1]
            if DEBUG and n == 0:
                wait("gpsimd", "vV", v_qh)
                for k in range(KI):
                    dma("gpsimd", dbg_qh[128 * k:128 * (k + 1), :], qh[:, TT * k:TT * (k + 1)], "qG")
            for grp in range(2):
                NBW = 8
                tM_base_d = cnt["tM"]
                q_wd = []
                f_wd = []
                for b in range(KI // NBW):
                    if b >= 2:
                        wait("sync", "tM", tM_base_d + (b - 1) * NBW)
                    qv = 0
                    for kk in range(NBW):
                        k = b * NBW + kk
                        qv = dma("sync", wd_t[b % 2][:, 1024 * kk:1024 * (kk + 1)],
                                 tdout.ap()[128 * k:128 * (k + 1),
                                            1024 * grp:1024 * (grp + 1)], "qS")
                    q_wd.append(qv)
                    f_wd.append(qfence("sync"))
                if v_outev is not None:
                    wait("tensor", "vV", v_outev)
                lastmm = None
                for b in range(KI // NBW):
                    wait("tensor", "vV", v_qh_marks[b * NBW + NBW - 1])
                    wait("tensor", "qS", q_wd[b])
                    wfence("tensor", f_wd[b])
                    for kk in range(NBW):
                        k = b * NBW + kk
                        for mloc in range(8):
                            def mm(t, k=k, kk=kk, b=b, mloc=mloc):
                                return t.matmul(pb[mloc][:],
                                                wd_t[b % 2][:, 1024 * kk + 128 * mloc:1024 * kk + 128 * (mloc + 1)],
                                                qh[:, TT * k:TT * (k + 1)],
                                                start=(k == 0), stop=(k == KI - 1))
                            if mloc == 7:
                                lastmm = op("tensor", mm, "tM")
                            else:
                                op("tensor", mm)
                t_dn_last = lastmm
                wait("vector", "tM", t_dn_last)
                q_x1c = 0
                for mloc in range(8):
                    m = 8 * grp + mloc
                    q_x1c = dma("sync", x1c[:, TT * mloc:TT * (mloc + 1)],
                                x1_sp[128 * m:128 * (m + 1), TT * n:TT * (n + 1)], "qS")
                f_x1c = qfence("sync")
                wait("vector", "qS", q_x1c)
                wfence("vector", f_x1c)
                for mloc in range(8):
                    m = 8 * grp + mloc
                    if estep >= 4:
                        wait("vector", "qS", out_marks[estep - 4])
                        if estep % 4 == 0:
                            wfence("vector", qfence("sync"))
                    fb = ffc[estep % 4]
                    op("vector", lambda v, mloc=mloc, n=n, fb=fb: v.scalar_tensor_tensor(
                        fb[:], pb[mloc][:], swv[:, 5:6], dq2b[:, TT * n:TT * (n + 1)],
                        ALU.mult, ALU.mult))
                    v_outev = op("vector", lambda v, m=m, mloc=mloc, fb=fb: v.scalar_tensor_tensor(
                        fb[:], fb[:], gbc[:, m:m + 1], x1c[:, TT * mloc:TT * (mloc + 1)],
                        ALU.mult, ALU.add), "vV")
                    wait("sync", "vV", v_outev)
                    out_marks.append(dma("sync", out_t[128 * m:128 * (m + 1), TT * n:TT * (n + 1)],
                                         fb[:], "qS"))
                    estep += 1
        q_final = cnt["qS"]
        p5.close()

        wait("sync", "qS", q_final)
        wait("gpsimd", "qG", cnt["qG"])

        block = E(nc.Block())

        @block.sync
        def _(e):
            for fn in plan["sync"]:
                fn(e)

        @block.gpsimd
        def _(e):
            for fn in plan["gpsimd"]:
                fn(e)

        @block.tensor
        def _(e):
            for fn in plan["tensor"]:
                fn(e)

        @block.vector
        def _(e):
            for fn in plan["vector"]:
                fn(e)

        @block.scalar
        def _(e):
            for fn in plan["scalar"]:
                fn(e)

    return nc


def _prep_inputs(inputs):
    f32 = np.float32
    x = np.asarray(inputs["L0_attn_hidden_states"])[0]
    xT = np.ascontiguousarray(x.T)
    wi = np.ascontiguousarray(np.asarray(inputs["L0_attn_w_i"], dtype=f32))
    wf = np.ascontiguousarray(np.asarray(inputs["L0_attn_w_f"], dtype=f32))
    wg = np.ascontiguousarray(np.asarray(inputs["L0_attn_w_g"], dtype=f32))
    woT = np.ascontiguousarray(np.asarray(inputs["L0_attn_w_o"]).T.astype(f32))
    wgT = np.ascontiguousarray(np.asarray(inputs["L0_ffn_w_gate"]).T.astype(f32))
    wuT = np.ascontiguousarray(np.asarray(inputs["L0_ffn_w_up"]).T.astype(f32))
    wdT = np.ascontiguousarray(np.asarray(inputs["L0_ffn_w_down"]).T.astype(f32))
    wmod = np.asarray(inputs["L0_wmod"], dtype=f32)

    def pk(v, K):
        return np.ascontiguousarray(np.asarray(v, dtype=f32).reshape(K, 128).T)

    base = {
        "wi": wi, "wf": wf, "wg": wg, "woT": woT,
        "c_pk": pk(np.asarray(inputs["L0_c"]).reshape(-1), KD),
        "si": pk(inputs["L0_attn_s_i"], KD), "sf": pk(inputs["L0_attn_s_f"], KD),
        "sg": pk(inputs["L0_attn_s_g"], KD), "so": pk(inputs["L0_attn_s_o"], KD),
        "gnw": pk(inputs["L0_attn_g_norm_w"], KD),
        "lnw": pk(inputs["L0_ffn_ln_weight"], KI),
        "ident": np.eye(128, dtype=f32),
    }
    in_maps = []
    for c in range(NC):
        m = dict(base)
        m["x_in"] = np.ascontiguousarray(xT[:, TC * c:TC * (c + 1)])
        m["wmod_sl"] = np.ascontiguousarray(wmod[:, MODS * c:MODS * (c + 1)])
        m["tsl_g"] = np.ascontiguousarray(wgT[(D // NC) * c:(D // NC) * (c + 1), :])
        m["tsl_u"] = np.ascontiguousarray(wuT[(D // NC) * c:(D // NC) * (c + 1), :])
        m["tsl_d"] = np.ascontiguousarray(wdT[(I // NC) * c:(I // NC) * (c + 1), :])
        cm0v = np.zeros((128, NC), f32)
        cm0v[:, :c] = 1.0
        m["cm0"] = np.ascontiguousarray(cm0v)
        m["cm1"] = np.ascontiguousarray(1.0 - cm0v)
        in_maps.append(m)
    return in_maps


def kernel(**inputs):
    if "nc" not in _CACHED:
        _CACHED["nc"] = build_nc()
    nc = _CACHED["nc"]
    in_maps = _prep_inputs(inputs)
    res = run_bass_kernel_spmd(nc, in_maps, core_ids=list(range(NC)))
    outs = [res.results[c]["out_t"] for c in range(NC)]
    full = np.concatenate(outs, axis=1)
    return np.ascontiguousarray(full.T)[None].astype(np.float32)


# version 16
# speedup vs baseline: 1.0478x; 1.0478x over previous
import numpy as np
from contextlib import ExitStack
import concourse.bass as bass
import concourse.mybir as mybir
from concourse.bass_utils import run_bass_kernel_spmd

F32 = mybir.dt.float32
F16 = mybir.dt.float16
BF16 = mybir.dt.bfloat16
AF = mybir.ActivationFunctionType
ALU = mybir.AluOpType
AX = mybir.AxisListType

MAGIC = 12582912.0
NC = 8
D = 2048
KD = D // 128
I = 8192
KI = I // 128
TC = 1024
TT = 512
NT = TC // TT
MODS = 6 * D // NC
FOLD_ORDERS = [
    [12 * r + j for r in range(8) for j in (0, 1)],
    [12 * r + j for r in range(8) for j in (2, 3)],
    [54, 19, 69, 78, 53, 46, 17, 42, 4, 28, 64, 44, 33, 21, 55, 16, 81, 66, 56, 45, 79, 90, 29, 71, 41, 57, 70, 32, 67, 94, 76, 6, 20, 52, 31, 82, 40, 30, 58, 35, 5, 10, 22, 80, 77, 68, 65, 9, 83, 11, 93, 91, 92, 88, 89, 18, 43, 95, 47, 23, 34, 8, 7, 59],
]
DEBUG = False

_CACHED = {}


def build_nc():
    nc = bass.Bass("TRN2", target_bir_lowering=False, debug=False)

    x_in = nc.dram_tensor("x_in", [D, TC], F32, kind="ExternalInput").ap()
    wi_d = nc.dram_tensor("wi", [D, D], F32, kind="ExternalInput").ap()
    wf_d = nc.dram_tensor("wf", [D, D], F32, kind="ExternalInput").ap()
    wg_d = nc.dram_tensor("wg", [D, D], F32, kind="ExternalInput").ap()
    woT = nc.dram_tensor("woT", [D, D], F32, kind="ExternalInput").ap()
    wmod_sl = nc.dram_tensor("wmod_sl", [D, MODS], F32, kind="ExternalInput").ap()
    tsl_g = nc.dram_tensor("tsl_g", [D // NC, I], F32, kind="ExternalInput").ap()
    tsl_u = nc.dram_tensor("tsl_u", [D // NC, I], F32, kind="ExternalInput").ap()
    tsl_d = nc.dram_tensor("tsl_d", [I // NC, D], F32, kind="ExternalInput").ap()
    c_pk = nc.dram_tensor("c_pk", [128, KD], F32, kind="ExternalInput").ap()
    si_i = nc.dram_tensor("si", [128, KD], F32, kind="ExternalInput").ap()
    sf_i = nc.dram_tensor("sf", [128, KD], F32, kind="ExternalInput").ap()
    sg_i = nc.dram_tensor("sg", [128, KD], F32, kind="ExternalInput").ap()
    so_i = nc.dram_tensor("so", [128, KD], F32, kind="ExternalInput").ap()
    gnw_i = nc.dram_tensor("gnw", [128, KD], F32, kind="ExternalInput").ap()
    lnw_i = nc.dram_tensor("lnw", [128, KI], F32, kind="ExternalInput").ap()
    cm0_i = nc.dram_tensor("cm0", [128, NC], F32, kind="ExternalInput").ap()
    cm1_i = nc.dram_tensor("cm1", [128, NC], F32, kind="ExternalInput").ap()
    id_i = nc.dram_tensor("ident", [128, 128], F32, kind="ExternalInput").ap()

    out_t = nc.dram_tensor("out_t", [D, TC], F32, kind="ExternalOutput").ap()

    spill_kind = "ExternalOutput" if DEBUG else "Internal"
    ag1_in = nc.dram_tensor("ag1_in", [1, MODS + 16], F32, kind="Internal").ap()
    ag1_out = nc.dram_tensor("ag1_out", [NC, MODS + 16], F32, kind="Internal").ap()
    tgin = nc.dram_tensor("tgin", [D // NC, I], BF16, kind="Internal")
    tgout = nc.dram_tensor("tgout", [D, I], BF16, kind="Internal")
    tuin = nc.dram_tensor("tuin", [D // NC, I], BF16, kind="Internal")
    tuout = nc.dram_tensor("tuout", [D, I], BF16, kind="Internal")
    tdin = nc.dram_tensor("tdin", [I // NC, D], BF16, kind="Internal")
    tdout = nc.dram_tensor("tdout", [I, D], BF16, kind="Internal")
    sum_in = nc.dram_tensor("sum_in", [2 * KD, 128], F32, kind="Internal").ap()
    sum_out = nc.dram_tensor("sum_out", [NC * 2 * KD, 128], F32, kind="Internal").ap()
    h_sp = nc.dram_tensor("h_sp", [D, TC], F16, kind=spill_kind).ap()
    p_sp = nc.dram_tensor("p_sp", [D, TC], F16, kind=spill_kind).ap()
    g_sp = nc.dram_tensor("g_sp", [D, TC], F16, kind=spill_kind).ap()
    x1_sp = nc.dram_tensor("x1_sp", [D, TC], F32, kind=spill_kind).ap()
    hf_sp = nc.dram_tensor("hf_sp", [I, TC], F16, kind=spill_kind).ap()
    if DEBUG:
        dbg_o16 = nc.dram_tensor("dbg_o16", [D, TC], F16, kind="ExternalOutput").ap()
        dbg_qx = nc.dram_tensor("dbg_qx", [D, TC], BF16, kind="ExternalOutput").ap()
        dbg_carry = nc.dram_tensor("dbg_carry", [128, KD], F32, kind="ExternalOutput").ap()
        dbg_rows = nc.dram_tensor("dbg_rows", [3, TC], F32, kind="ExternalOutput").ap()
        dbg_att = nc.dram_tensor("dbg_att", [D, TC], F32, kind="ExternalOutput").ap()
        dbg_adc = nc.dram_tensor("dbg_adc", [128, 96 + 2 * KD], F32, kind="ExternalOutput").ap()
        dbg_xf = nc.dram_tensor("dbg_xf", [D, TC], F16, kind="ExternalOutput").ap()
        dbg_fi = nc.dram_tensor("dbg_fi", [2 * 128, TC], F32, kind="ExternalOutput").ap()
        dbg_qh = nc.dram_tensor("dbg_qh", [I, TT], BF16, kind="ExternalOutput").ap()
        dbg_r2 = nc.dram_tensor("dbg_r2", [4, TC], F32, kind="ExternalOutput").ap()
        dbg_td = nc.dram_tensor("dbg_td", [I // NC, D], BF16, kind="ExternalOutput").ap()
        dbg_tg = nc.dram_tensor("dbg_tg", [D // NC, I], BF16, kind="ExternalOutput").ap()

    plan = {k: [] for k in ("sync", "gpsimd", "tensor", "vector", "scalar")}
    sems = {}
    cnt = {}

    with ExitStack() as top:
        E = top.enter_context
        for name in ("qS", "qG", "tM", "vV", "aA", "cC"):
            sems[name] = E(nc.semaphore(name))
            cnt[name] = 0

        def em(eng, fn):
            plan[eng].append(fn)

        def dma(eng, out, in_, q, slow=False):
            cnt[q] += 16
            s = sems[q]
            if slow:
                em(eng, lambda e, out=out, in_=in_, s=s: e.dma_start(
                    out=out, in_=in_, allow_slow_non_contiguous=True).then_inc(s, 16))
            else:
                em(eng, lambda e, out=out, in_=in_, s=s: e.dma_start(out=out, in_=in_).then_inc(s, 16))
            return cnt[q]

        def wait(eng, q, v):
            if v is None or v <= 0:
                return
            s = sems[q]
            em(eng, lambda e, s=s, v=v: e.wait_ge(s, v))

        def op(eng, fn, q=None, k=1):
            if q is None:
                em(eng, lambda e, f=fn: f(e))
                return None
            cnt[q] += k
            s = sems[q]
            em(eng, lambda e, f=fn, s=s, k=k: f(e).then_inc(s, k))
            return cnt[q]

        def fence(eng):
            q = {"vector": "vV", "scalar": "aA"}[eng]
            return op(eng, lambda e: e.engine_nop(), q)

        def coll(src, dst):
            return op("gpsimd", lambda g, s=src, d=dst: g.collective_compute(
                "AllGather", ALU.bypass, replica_groups=[list(range(NC))],
                ins=[s], outs=[d]), "cC")

        def barrier_dma():
            for e in ("sync", "gpsimd"):
                wait(e, "tM", cnt["tM"])
                wait(e, "vV", cnt["vV"])
                wait(e, "aA", cnt["aA"])

        ones16 = E(nc.sbuf_tensor("ones16", [128, 128], F16))
        ones32 = E(nc.sbuf_tensor("ones32", [128, 128], F32))
        ones1 = E(nc.sbuf_tensor("ones1f", [1, 128], F32))
        onescol = E(nc.sbuf_tensor("onescol", [128, 1], F32))
        ident = E(nc.sbuf_tensor("identt", [128, 128], F32))
        sivec = E(nc.sbuf_tensor("sivec", [128, KD], F32))
        sfvec = E(nc.sbuf_tensor("sfvec", [128, KD], F32))
        sgvec = E(nc.sbuf_tensor("sgvec", [128, KD], F32))
        gnwso = E(nc.sbuf_tensor("gnwso", [128, KD], F32))
        lnwv = E(nc.sbuf_tensor("lnwv", [128, KI], F32))
        lnw2 = E(nc.sbuf_tensor("lnw2", [128, KI], F32))
        cm0 = E(nc.sbuf_tensor("cm0s", [128, NC], F32))
        cm1 = E(nc.sbuf_tensor("cm1s", [128, NC], F32))
        adc = E(nc.sbuf_tensor("adc", [128, 96], F32))
        ca256 = E(nc.sbuf_tensor("ca256", [128, KD], F32))
        sa256 = E(nc.sbuf_tensor("sa256", [128, KD], F32))
        cb1 = E(nc.sbuf_tensor("cb1", [128, KD], F32))
        sb_c = E(nc.sbuf_tensor("sb_c", [128, KD], F32))
        gac = E(nc.sbuf_tensor("gac", [128, KD], F32))
        gbc = E(nc.sbuf_tensor("gbc", [128, KD], F32))
        swv = E(nc.sbuf_tensor("swv", [128, 8], F32))
        carry = E(nc.sbuf_tensor("carry", [128, KD], F32))
        scv = E(nc.sbuf_tensor("scv", [128, KD], F32))
        tiny = E(nc.sbuf_tensor("tiny", [128, KD], F32))
        rowA = E(nc.sbuf_tensor("rowA", [1, TC], F32))
        rowB = E(nc.sbuf_tensor("rowB", [1, TC], F32))
        dqrow = E(nc.sbuf_tensor("dqrow", [1, TC], F32))
        c2row = E(nc.sbuf_tensor("c2row", [1, TC], F32))
        dq2row = E(nc.sbuf_tensor("dq2row", [1, TC], F32))
        zeros = E(nc.sbuf_tensor("zeros", [128, TT], F32))
        dqb = E(nc.sbuf_tensor("dqb", [128, TC], F32))
        c2b = E(nc.sbuf_tensor("c2b", [128, TC], F32))
        dq2b = E(nc.sbuf_tensor("dq2b", [128, TC], F32))
        tokm = E(nc.sbuf_tensor("tokm", [128, 16], F32))
        tokr = E(nc.sbuf_tensor("tokr", [16, 128], F32))
        gnwtmp = E(nc.sbuf_tensor("gnwtmp", [128, KD], F32))
        epsc = E(nc.sbuf_tensor("epsc", [128, 2], F32))

        pb = [E(nc.psum_tensor(f"pb{i}", [128, 512], F32)) for i in range(8)]
        fsrc = E(nc.sbuf_tensor("fsrc", [128, 16], F32))
        fdst = E(nc.sbuf_tensor("fdst", [128, 16], F32))
        NFR = 12
        fring = {q: [E(nc.semaphore(f"fr_{q}_{i}")) for i in range(NFR)]
                 for q in ("sync", "gpsimd")}
        fr_idx = {"sync": 0, "gpsimd": 0}
        fr_cnt = {}

        def qfence(eng):
            i = fr_idx[eng] % NFR
            fr_idx[eng] += 1
            key = (eng, i)
            fr_cnt[key] = fr_cnt.get(key, 0) + 16
            sm = fring[eng][i]
            em(eng, lambda e, sm=sm: e.dma_start(out=fdst[:], in_=fsrc[:]).then_inc(sm, 16))
            return (eng, i, fr_cnt[key])

        def wfence(waiter, tok):
            eng, i, v = tok
            em(waiter, lambda e, sm=fring[eng][i], v=v: e.wait_ge(sm, v))

        v_init = op("vector", lambda v: v.memset(fsrc[:], 0.0), "vV")
        wait("sync", "vV", v_init)
        wait("gpsimd", "vV", v_init)
        op("vector", lambda v: v.memset(ones16[:], 1.0))
        op("vector", lambda v: v.memset(ones32[:], 1.0))
        op("vector", lambda v: v.memset(ones1[:], 1.0))
        op("vector", lambda v: v.memset(onescol[:], 1.0))
        op("vector", lambda v: v.memset(zeros[:], 0.0))
        op("vector", lambda v: v.memset(epsc[:, 0:1], 1e-5))
        op("vector", lambda v: v.memset(epsc[:, 1:2], 1e-6))
        q_cpk = dma("sync", tiny[:], c_pk, "qS")
        q_small = q_cpk
        for dst, src in ((sivec, si_i), (sfvec, sf_i), (sgvec, sg_i),
                         (gnwso, so_i), (lnwv, lnw_i), (cm0, cm0_i), (cm1, cm1_i),
                         (ident, id_i)):
            q_small = dma("sync", dst[:], src, "qS")
        q_small = dma("sync", gnwtmp[:], gnw_i, "qS")

        f_small = qfence("sync")
        wait("scalar", "qS", q_small)
        wfence("scalar", f_small)
        a_siluc = op("scalar", lambda s: s.activation(scv[:], tiny[:], AF.Silu), "aA")
        wait("vector", "qS", q_small)
        wfence("vector", f_small)
        op("vector", lambda v: v.tensor_mul(gnwso[:], gnwso[:], gnwtmp[:]))
        op("vector", lambda v: v.tensor_mul(lnw2[:], lnwv[:], lnwv[:]))
        op("vector", lambda v: v.drain())

        def part_max_row(src_t, width, pbank, rowdst, v_ready):
            nb = width // 128
            wait("tensor", "vV", v_ready)
            for b in range(nb):
                t_tr = op("tensor", lambda t, b=b: t.transpose(
                    pb[pbank][:, 0:128], src_t[:, 128 * b:128 * (b + 1)], ident[:]), "tM")
                wait("vector", "tM", t_tr)
                op("vector", lambda v, b=b: v.tensor_reduce(
                    tokm[:, b:b + 1], pb[pbank][:, 0:128], AX.X, ALU.max))
                vm = fence("vector")
                if b < nb - 1:
                    wait("tensor", "vV", vm)
            op("vector", lambda v: v.drain())
            v_tok = fence("vector")
            wait("tensor", "vV", v_tok)
            t_tb = op("tensor", lambda t, nb=nb: t.transpose(
                pb[pbank][0:nb, 0:128], tokm[:, 0:nb], ident[:]), "tM")
            wait("vector", "tM", t_tb)
            op("vector", lambda v, nb=nb: v.tensor_copy(tokr[0:nb, 0:128], pb[pbank][0:nb, 0:128]))
            v_tb = fence("vector")
            wait("gpsimd", "vV", v_tb)
            q_row = 0
            for b in range(nb):
                q_row = dma("gpsimd", rowdst[0:1, 128 * b:128 * (b + 1)], tokr[b:b + 1, :], "qG")
            f_row = qfence("gpsimd")
            wait("vector", "qG", q_row)
            wfence("vector", f_row)
            return fence("vector")

        p01 = ExitStack()
        E0 = p01.enter_context
        wmt = [E0(nc.sbuf_tensor(f"wmt{i}", [128, MODS], F32)) for i in range(2)]
        modrow = E0(nc.sbuf_tensor("modrow", [1, MODS], F32))
        sl_t = [E0(nc.sbuf_tensor(f"slt{i}", [128, I], F32)) for i in range(2)]
        partial = E0(nc.sbuf_tensor("partial", [128, 16], F32))
        psum1 = E0(nc.sbuf_tensor("psum1", [128, 4], F32))
        sumrow = E0(nc.sbuf_tensor("sumrow", [1, 12], F32))
        sumg = E0(nc.sbuf_tensor("sumg", [128, 4], F32))
        slotrow = E0(nc.sbuf_tensor("slotrow", [1, 96], F32))
        mrow = E0(nc.sbuf_tensor("mrow", [1, 12], F32))
        tern_t = [E0(nc.sbuf_tensor(f"tern{i}", [128, I], BF16)) for i in range(2)]

        NMOD = MODS // 512
        tM_base0 = cnt["tM"]
        q_wm = []
        fw_wm = []
        for k in range(KD):
            if k >= 2:
                wait("gpsimd", "tM", tM_base0 + k - 1)
            q_wm.append(dma("gpsimd", wmt[k % 2][:], wmod_sl[128 * k:128 * (k + 1), :], "qG"))
            fw_wm.append(qfence("gpsimd"))
        wait("tensor", "aA", a_siluc)
        t_mod = None
        for k in range(KD):
            wait("tensor", "qG", q_wm[k])
            wfence("tensor", fw_wm[k])
            for n in range(NMOD):
                def mm(t, k=k, n=n):
                    return t.matmul(pb[n][0:1, :], scv[:, k:k + 1],
                                    wmt[k % 2][:, 512 * n:512 * (n + 1)],
                                    start=(k == 0), stop=(k == KD - 1))
                t_mod = op("tensor", mm, "tM") if n == NMOD - 1 else op("tensor", mm)
        wait("vector", "tM", t_mod)
        for n in range(NMOD):
            op("vector", lambda v, n=n: v.tensor_copy(modrow[:, 512 * n:512 * (n + 1)], pb[n][0:1, :]))
        v_mod = fence("vector")
        wait("sync", "vV", v_mod)
        dma("sync", ag1_in[0:1, 0:MODS], modrow[:], "qS")

        slot_marks = []
        tiles = []
        for w_ap, ntile, width in ((tsl_g, 2, I), (tsl_u, 2, I), (tsl_d, 8, D)):
            for j in range(ntile):
                tiles.append((w_ap, j, width))
        for idx, (w_ap, j, width) in enumerate(tiles):
            if idx >= 2:
                wait("sync", "vV", slot_marks[idx - 2])
            qld = dma("sync", sl_t[idx % 2][:, 0:width],
                      w_ap[128 * j:128 * (j + 1), :], "qS")
            wfence("vector", qfence("sync"))
            wait("vector", "qS", qld)
            slot_marks.append(op("vector", lambda v, idx=idx, width=width: v.tensor_reduce(
                partial[:, idx:idx + 1], sl_t[idx % 2][:, 0:width], AX.X, ALU.add,
                apply_absolute_value=True), "vV"))
        op("vector", lambda v: v.drain())
        v_part = fence("vector")
        wait("tensor", "vV", v_part)
        t_psum = op("tensor", lambda t: t.matmul(pb[4][0:1, 0:12], onescol[:],
                                                 partial[:, 0:12], start=True, stop=True), "tM")
        wait("vector", "tM", t_psum)
        op("vector", lambda v: v.tensor_copy(sumrow[:, 0:12], pb[4][0:1, 0:12]))
        v_sum = fence("vector")
        wait("sync", "vV", v_sum)
        q_sum_out = dma("sync", ag1_in[0:1, MODS:MODS + 12], sumrow[:, 0:12], "qS")

        wait("gpsimd", "qS", q_sum_out)
        wfence("gpsimd", qfence("sync"))
        c_ag1 = coll(ag1_in, ag1_out)

        wait("gpsimd", "cC", c_ag1)
        q_un = 0
        for r in range(NC):
            src = ag1_out[r:r + 1, 0:MODS].rearrange("o (q p) -> (o p) q", p=128)
            q_un = dma("gpsimd", adc[:, 12 * r:12 * (r + 1)], src, "qG", slow=True)
        for r in range(NC):
            dma("gpsimd", slotrow[0:1, 12 * r:12 * (r + 1)], ag1_out[r:r + 1, MODS:MODS + 12], "qG")
        q_un = dma("gpsimd", sumg[0:NC, 0:3], ag1_out[0:NC, MODS:MODS + 3], "qG")
        fv_un = qfence("gpsimd")

        wait("vector", "qG", q_un)
        wfence("vector", fv_un)
        op("vector", lambda v: v.tensor_scalar(ca256[:], adc[:, KD:2 * KD], 256.0, 256.0, ALU.mult, ALU.add))
        op("vector", lambda v: v.tensor_scalar(sa256[:], adc[:, 0:KD], 256.0, None, ALU.mult))
        op("vector", lambda v: v.tensor_scalar(cb1[:], adc[:, 4 * KD:5 * KD], 1.0, None, ALU.add))
        op("vector", lambda v: v.tensor_copy(sb_c[:], adc[:, 3 * KD:4 * KD]))
        op("vector", lambda v: v.drain())
        for gidx, gdst in ((2, gac), (5, gbc)):
            gsrc = adc[:, gidx * KD:(gidx + 1) * KD]
            op("vector", lambda v, s=gsrc, d=gdst: v.tensor_single_scalar(d[:], s, 0.0, ALU.is_ge))
            op("vector", lambda v: v.drain())
            op("vector", lambda v, d=gdst: v.tensor_scalar(d[:], d[:], 2.0, -1.0, ALU.mult, ALU.add))
            op("vector", lambda v: v.drain())
            op("vector", lambda v, s=gsrc: v.scalar_tensor_tensor(tiny[:], s, -1.0, s, ALU.mult, ALU.max))
            op("vector", lambda v: v.drain())
            op("vector", lambda v: v.tensor_scalar_max(tiny[:], tiny[:], 0.25))
            op("vector", lambda v: v.drain())
            op("vector", lambda v, d=gdst: v.tensor_mul(d[:], d[:], tiny[:]))
            op("vector", lambda v: v.drain())
        v_adaln = fence("vector")
        if DEBUG:
            wait("gpsimd", "vV", v_adaln)
            dma("gpsimd", dbg_adc[:, 0:96], adc[:], "qG")
            dma("gpsimd", dbg_adc[:, 96:96 + KD], gac[:], "qG")
            dma("gpsimd", dbg_adc[:, 96 + KD:96 + 2 * KD], gbc[:], "qG")
        wait("tensor", "qG", q_un)
        wfence("tensor", fv_un)
        for wslot, order in enumerate(FOLD_ORDERS):
            op("vector", lambda v, o=order[0], w=wslot: v.tensor_copy(
                mrow[0:1, 6 + w:7 + w], slotrow[0:1, o:o + 1]))
            op("vector", lambda v: v.drain())
            for o in order[1:]:
                op("vector", lambda v, o=o, w=wslot: v.tensor_add(
                    mrow[0:1, 6 + w:7 + w], mrow[0:1, 6 + w:7 + w], slotrow[0:1, o:o + 1]))
                op("vector", lambda v: v.drain())
        v_fold = fence("vector")
        if DEBUG:
            wait("gpsimd", "vV", v_fold)
            dma("gpsimd", dbg_r2[3:4, 8:104], slotrow[0:1, 0:96], "qG")
            dma("gpsimd", dbg_r2[3:4, 104:116], mrow[0:1, 0:12], "qG")
        op("vector", lambda v: v.tensor_scalar(mrow[:, 3:6], mrow[0:1, 6:9], 1.0 / (D * I), 1e-5,
                                               ALU.mult, ALU.max))
        op("vector", lambda v: v.drain())
        op("vector", lambda v: v.reciprocal(mrow[:, 0:3], mrow[:, 3:6]))
        op("vector", lambda v: v.drain())
        v_mrow = fence("vector")
        wait("tensor", "vV", v_mrow)
        t_swb = op("tensor", lambda t: t.matmul(pb[5][:, 0:8], ones1[:],
                                                mrow[:, 0:8], start=True, stop=True), "tM")
        wait("vector", "tM", t_swb)
        op("vector", lambda v: v.tensor_copy(swv[:], pb[5][:, 0:8]))
        v_swv = fence("vector")

        tern_jobs = (("g", tsl_g, tgin, tgout, 2, I, 0), ("u", tsl_u, tuin, tuout, 2, I, 1),
                     ("d", tsl_d, tdin, tdout, 8, D, 2))
        q_t_out = 0
        idx = len(tiles)
        first = True
        for namew, w_ap, t_in, t_out, ntile, width, scol in tern_jobs:
            for j in range(ntile):
                wait("sync", "vV", slot_marks[idx - 2])
                qld = dma("sync", sl_t[idx % 2][:, 0:width], w_ap[128 * j:128 * (j + 1), :], "qS")
                wfence("vector", qfence("sync"))
                wait("vector", "qS", qld)
                if first:
                    wait("vector", "vV", v_swv)
                    first = False
                if idx >= len(tiles) + 2:
                    wfence("vector", qg_marks[idx - len(tiles) - 2])
                op("vector", lambda v, i=idx, w=width, sc=scol: v.tensor_scalar(
                    sl_t[i % 2][:, 0:w], sl_t[i % 2][:, 0:w], swv[:, sc:sc + 1], MAGIC,
                    ALU.mult, ALU.add))
                op("vector", lambda v, i=idx, w=width: v.tensor_scalar(
                    sl_t[i % 2][:, 0:w], sl_t[i % 2][:, 0:w], -MAGIC, -1.0,
                    ALU.add, ALU.max))
                slot_marks.append(op("vector", lambda v, i=idx, w=width: v.tensor_single_scalar(
                    tern_t[i % 2][:, 0:w], sl_t[i % 2][:, 0:w], 1.0, ALU.min), "vV"))
                wait("gpsimd", "vV", slot_marks[-1])
                q_t_out = dma("gpsimd", t_in.ap()[128 * j:128 * (j + 1), :],
                              tern_t[idx % 2][:, 0:width], "qG")
                if 'qg_marks' not in locals():
                    qg_marks = []
                qg_marks.append(qfence("gpsimd"))
                if DEBUG and namew == "d":
                    q_t_out = dma("gpsimd", dbg_td[128 * j:128 * (j + 1), :],
                                  tern_t[idx % 2][:, 0:width], "qG")
                if DEBUG and namew == "g":
                    q_t_out = dma("gpsimd", dbg_tg[128 * j:128 * (j + 1), :],
                                  tern_t[idx % 2][:, 0:width], "qG")
                idx += 1
            if namew in ("g", "u"):
                wait("gpsimd", "qG", q_t_out)
                wfence("gpsimd", qfence("gpsimd"))
                coll(t_in.ap(), t_out.ap())
        c_ag2u = cnt["cC"]
        q_tern_done = q_t_out

        p01.close()
        barrier_dma()
        wait("vector", "qG", cnt["qG"])

        pxf = ExitStack()
        xf16 = pxf.enter_context(nc.sbuf_tensor("xf16", [128, KD * TC], F16))

        pln = ExitStack()
        EL = pln.enter_context
        xint = EL(nc.sbuf_tensor("xint", [128, KD * TT], F32))
        xsq32 = EL(nc.sbuf_tensor("xsq32", [128, KD * TT], F32))
        mrep = EL(nc.sbuf_tensor("mrep", [128, TT], F32))
        rstd = EL(nc.sbuf_tensor("rstd", [128, TT], F32))
        vtmp = EL(nc.sbuf_tensor("vtmp", [128, TT], F32))
        ttmp = EL(nc.sbuf_tensor("ttmp", [128, TT], F32))

        v_ln_done = None
        t_stats_prev = None
        a_sq = None
        v_var_mark = None
        for n in range(NT):
            if n > 0:
                wait("sync", "vV", v_ln_done)
                wait("sync", "tM", t_stats_prev)
            q_x = 0
            for k in range(KD):
                q_x = dma("sync", xint[:, TT * k:TT * (k + 1)],
                          x_in[128 * k:128 * (k + 1), TT * n:TT * (n + 1)], "qS")
            f_x = qfence("sync")
            wait("scalar", "qS", q_x)
            wfence("scalar", f_x)
            a_sq = op("scalar", lambda s: s.activation(xsq32[:], xint[:], AF.Square), "aA")
            wait("tensor", "qS", q_x)
            wfence("tensor", f_x)
            if n > 0:
                wait("tensor", "vV", v_var_mark)
            for k in range(KD):
                op("tensor", lambda t, k=k: t.matmul(
                    pb[6][:], ones32[:], xint[:, TT * k:TT * (k + 1)],
                    start=(k == 0), stop=(k == KD - 1)))
            wait("tensor", "aA", a_sq)
            t_stats = None
            for k in range(KD):
                def mm(t, k=k):
                    return t.matmul(pb[7][:], ones32[:], xsq32[:, TT * k:TT * (k + 1)],
                                    start=(k == 0), stop=(k == KD - 1))
                t_stats = op("tensor", mm, "tM") if k == KD - 1 else op("tensor", mm)
            t_stats_prev = t_stats
            wait("vector", "tM", t_stats)
            op("vector", lambda v: v.tensor_scalar(mrep[:], pb[6][:], 1.0 / D, None, ALU.mult))
            op("vector", lambda v: v.tensor_scalar(vtmp[:], pb[7][:], 1.0 / D, None, ALU.mult))
            op("vector", lambda v: v.scalar_tensor_tensor(ttmp[:], mrep[:], -1.0, mrep[:],
                                                          ALU.mult, ALU.mult))
            op("vector", lambda v: v.tensor_add(vtmp[:], vtmp[:], ttmp[:]))
            v_var_mark = fence("vector")
            wait("scalar", "vV", v_var_mark)
            op("scalar", lambda s: s.activation(ttmp[:], vtmp[:], AF.Ln, bias=epsc[:, 0:1]))
            a_rstd = op("scalar", lambda s: s.activation(rstd[:], ttmp[:], AF.Exp, scale=-0.5), "aA")
            wait("vector", "aA", a_rstd)
            wait("vector", "qS", q_x)
            wfence("vector", f_x)
            for k in range(KD):
                xs = xint[:, TT * k:TT * (k + 1)]
                dstf = xf16[:, TC * k + TT * n: TC * k + TT * (n + 1)]
                op("vector", lambda v, xs=xs: v.tensor_sub(ttmp[:], xs, mrep[:]))
                op("vector", lambda v: v.tensor_mul(ttmp[:], ttmp[:], rstd[:]))
                op("vector", lambda v, k=k: v.tensor_scalar(
                    ttmp[:], ttmp[:], ca256[:, k:k + 1], sa256[:, k:k + 1], ALU.mult, ALU.add))
                op("vector", lambda v: v.tensor_scalar(
                    ttmp[:], ttmp[:], MAGIC, -MAGIC, ALU.add, ALU.add))
                op("vector", lambda v, d=dstf: v.tensor_scalar(
                    d, ttmp[:], 1.0 / 256.0, None, ALU.mult))
            v_ln_done = fence("vector")
        v_ln_all = v_ln_done
        if DEBUG:
            wait("gpsimd", "vV", v_ln_all)
            for k in range(KD):
                dma("gpsimd", dbg_xf[128 * k:128 * (k + 1), :], xf16[:, TC * k:TC * (k + 1)], "qG")
        pln.close()
        barrier_dma()

        psw = ExitStack()
        ES = psw.enter_context
        wsl = [ES(nc.sbuf_tensor(f"wsl{i}", [128, 3 * KD * 512], F16)) for i in range(2)]
        f_m = ES(nc.sbuf_tensor("f_m", [128, TC], F32))
        i_m = ES(nc.sbuf_tensor("i_m", [128, TC], F32))
        t_m = ES(nc.sbuf_tensor("t_m", [128, TC], F32))
        h_m = ES(nc.sbuf_tensor("h_m", [128, TC], F32))
        P_m = ES(nc.sbuf_tensor("P_m", [128, TC], F32))
        g16m = ES(nc.sbuf_tensor("g16m", [128, TC], F16))

        w_aps = (wi_d, wf_d, wg_d)
        NG = 4
        t_grp = []
        q_grp = []
        f_grp = []
        a_evac = []
        v_scan = []
        q_spill = []
        f_spill = []

        def wslice(buf, w, k):
            base = (w * KD + k) * 512
            return buf[:, base:base + 512]

        for grp in range(NG + 1):
            if grp < NG:
                if grp >= 2:
                    wait("gpsimd", "tM", t_grp[grp - 2])
                qv = 0
                for w in range(3):
                    for k in range(KD):
                        qv = dma("gpsimd", wslice(wsl[grp % 2], w, k),
                                 w_aps[w][128 * k:128 * (k + 1), 512 * grp:512 * (grp + 1)], "qG")
                q_grp.append(qv)
                f_grp.append(qfence("gpsimd"))
            if grp == 0:
                continue
            g = grp - 1
            wait("tensor", "qG", q_grp[g])
            wfence("tensor", f_grp[g])
            if g == 0:
                wait("tensor", "vV", v_ln_all)
            for mloc in range(4):
                m = 4 * g + mloc
                if m > 0:
                    wait("tensor", "aA", a_evac[m - 1])
                lastmm = None
                wmark = {}
                for w in range(3):
                    for n in range(NT):
                        for k in range(KD):
                            def mm(t, w=w, n=n, k=k, mloc=mloc, g=g):
                                return t.matmul(
                                    pb[2 * w + n][:],
                                    wslice(wsl[g % 2], w, k)[:, 128 * mloc:128 * (mloc + 1)],
                                    xf16[:, TC * k + TT * n:TC * k + TT * (n + 1)],
                                    start=(k == 0), stop=(k == KD - 1))
                            if n == NT - 1 and k == KD - 1:
                                lastmm = op("tensor", mm, "tM")
                                wmark[w] = lastmm
                            else:
                                op("tensor", mm)
                if mloc == 3:
                    t_grp.append(lastmm)
                wait("scalar", "tM", wmark[0])
                if m > 0:
                    wait("scalar", "vV", v_scan[m - 1])
                    wfence("scalar", f_spill[m - 1])
                for n in range(NT):
                    op("scalar", lambda s, m=m, n=n: s.activation(
                        i_m[:, TT * n:TT * (n + 1)], pb[0 + n][:], AF.Copy,
                        scale=sivec[:, m:m + 1]))
                wait("scalar", "tM", wmark[1])
                for n in range(NT):
                    op("scalar", lambda s, m=m, n=n: s.activation(
                        f_m[:, TT * n:TT * (n + 1)], pb[2 + n][:], AF.Sigmoid,
                        scale=sfvec[:, m:m + 1]))
                wait("scalar", "tM", wmark[2])
                a_last = None
                for n in range(NT):
                    a_last = op("scalar", lambda s, m=m, n=n: s.activation(
                        g16m[:, TT * n:TT * (n + 1)], pb[4 + n][:], AF.Copy,
                        scale=sgvec[:, m:m + 1]), "aA")
                a_evac.append(a_last)
                wait("gpsimd", "aA", a_last)
                dma("gpsimd", g_sp[128 * m:128 * (m + 1), :], g16m[:], "qG")
                wait("vector", "aA", a_last)
                if m > 0:
                    wfence("vector", f_spill[m - 1])
                op("vector", lambda v: v.tensor_scalar(t_m[:], f_m[:], -1.0, 1.0, ALU.mult, ALU.add))
                op("vector", lambda v: v.tensor_mul(i_m[:], t_m[:], i_m[:]))
                op("vector", lambda v: v.tensor_tensor_scan(
                    P_m[:, 0:TT], f_m[:, 0:TT], zeros[:], 1.0, ALU.mult, ALU.add))
                op("vector", lambda v: v.drain())
                op("vector", lambda v: v.tensor_tensor_scan(
                    P_m[:, TT:TC], f_m[:, TT:TC], zeros[:], P_m[:, TT - 1:TT], ALU.mult, ALU.add))
                op("vector", lambda v: v.tensor_tensor_scan(
                    h_m[:, 0:TT], f_m[:, 0:TT], i_m[:, 0:TT], 0.0, ALU.mult, ALU.add))
                op("vector", lambda v: v.drain())
                op("vector", lambda v: v.tensor_tensor_scan(
                    h_m[:, TT:TC], f_m[:, TT:TC], i_m[:, TT:TC], h_m[:, TT - 1:TT],
                    ALU.mult, ALU.add))
                v_scan.append(fence("vector"))
                wait("gpsimd", "vV", v_scan[m])
                if DEBUG and m == 0:
                    dma("gpsimd", dbg_fi[0:128, :], f_m[:], "qG")
                    dma("gpsimd", dbg_fi[128:256, :], i_m[:], "qG")
                dma("gpsimd", h_sp[128 * m:128 * (m + 1), :], h_m[:], "qG")
                dma("gpsimd", p_sp[128 * m:128 * (m + 1), :], P_m[:], "qG")
                dma("gpsimd", sum_in[m:m + 1, :], P_m[:, TC - 1:TC], "qG")
                q_spill.append(dma("gpsimd", sum_in[KD + m:KD + m + 1, :], h_m[:, TC - 1:TC], "qG"))
                f_spill.append(qfence("gpsimd"))
        q_p1_spill = cnt["qG"]
        wait("gpsimd", "qG", q_p1_spill)
        fv3 = qfence("gpsimd")
        wfence("gpsimd", fv3)

        c_ag3 = coll(sum_in, sum_out)
        coll(tdin.ap(), tdout.ap())
        c_ag2d = cnt["cC"]

        wait("gpsimd", "cC", c_ag3)
        Pj = ES(nc.sbuf_tensor("Pj", [128, KD], F32))
        Hj = ES(nc.sbuf_tensor("Hj", [128, KD], F32))
        sAb = ES(nc.sbuf_tensor("sAb", [128, KD], F32))
        sBb = ES(nc.sbuf_tensor("sBb", [128, KD], F32))
        op("vector", lambda v: v.memset(sAb[:], 0.0))
        cur, nxt = sAb, sBb
        for j in range(NC):
            dma("gpsimd", Pj[:], sum_out[32 * j:32 * j + KD, :].rearrange("m p -> p m"), "qG", slow=True)
            qh_ = dma("gpsimd", Hj[:], sum_out[32 * j + KD:32 * j + 2 * KD, :].rearrange("m p -> p m"), "qG", slow=True)
            fvj = qfence("gpsimd")
            wait("vector", "qG", qh_)
            wfence("vector", fvj)
            op("vector", lambda v, j=j: v.tensor_scalar(
                Pj[:], Pj[:], cm0[:, j:j + 1], cm1[:, j:j + 1], ALU.mult, ALU.add))
            op("vector", lambda v, j=j: v.tensor_scalar(
                Hj[:], Hj[:], cm0[:, j:j + 1], None, ALU.mult))
            op("vector", lambda v: v.drain())
            op("vector", lambda v, c=cur: v.tensor_mul(tiny[:], Pj[:], c[:]))
            op("vector", lambda v: v.drain())
            op("vector", lambda v, nx=nxt: v.tensor_add(nx[:], tiny[:], Hj[:]))
            op("vector", lambda v: v.drain())
            vj = fence("vector")
            wait("gpsimd", "vV", vj)
            cur, nxt = nxt, cur
        op("vector", lambda v: v.drain())
        op("vector", lambda v, c=cur: v.tensor_copy(carry[:], c[:]))
        op("vector", lambda v: v.drain())
        v_carry = fence("vector")
        if DEBUG:
            wait("gpsimd", "vV", v_carry)
            dma("gpsimd", dbg_carry, carry[:], "qG")
        psw.close()
        pxf.close()
        barrier_dma()

        po = ExitStack()
        o16 = po.enter_context(nc.sbuf_tensor("o16", [128, KD * TC], F16))
        p2a = ExitStack()
        E2 = p2a.enter_context
        h16s = E2(nc.sbuf_tensor("h16s", [128, KD * TT], F16))
        P16s = E2(nc.sbuf_tensor("P16s", [128, KD * TT], F16))
        g16s = E2(nc.sbuf_tensor("g16s", [128, KD * TT], F16))
        hw = E2(nc.sbuf_tensor("hw", [128, KD * TT], F32))
        hsq16 = E2(nc.sbuf_tensor("hsq16", [128, KD * TT], F16))
        sg16 = E2(nc.sbuf_tensor("sg16", [128, KD * TT], F16))
        rmsr = E2(nc.sbuf_tensor("rmsr", [128, TT], F32))
        vt2 = E2(nc.sbuf_tensor("vt2", [128, TT], F32))

        wait("sync", "qG", q_p1_spill)
        v_o16 = None
        a_rms_prev = None
        t_rms_prev = None
        for n in range(NT):
            if n > 0:
                wait("sync", "vV", v_o16)
                wait("sync", "tM", t_rms_prev)
            q_h = 0
            for k in range(KD):
                dma("sync", h16s[:, TT * k:TT * (k + 1)],
                    h_sp[128 * k:128 * (k + 1), TT * n:TT * (n + 1)], "qS")
                dma("sync", P16s[:, TT * k:TT * (k + 1)],
                    p_sp[128 * k:128 * (k + 1), TT * n:TT * (n + 1)], "qS")
                q_h = dma("sync", g16s[:, TT * k:TT * (k + 1)],
                          g_sp[128 * k:128 * (k + 1), TT * n:TT * (n + 1)], "qS")
            f_h = qfence("sync")
            wait("vector", "qS", q_h)
            wfence("vector", f_h)
            if n == 0:
                wait("vector", "vV", v_carry)
            for k in range(KD):
                op("vector", lambda v, k=k: v.scalar_tensor_tensor(
                    hw[:, TT * k:TT * (k + 1)], P16s[:, TT * k:TT * (k + 1)],
                    carry[:, k:k + 1], h16s[:, TT * k:TT * (k + 1)], ALU.mult, ALU.add))
            v_fix = fence("vector")
            wait("scalar", "vV", v_fix)
            a_hsq = op("scalar", lambda s: s.activation(hsq16[:], hw[:], AF.Square), "aA")
            wait("tensor", "aA", a_hsq)
            if n > 0:
                wait("tensor", "aA", a_rms_prev)
            t_rms = None
            for k in range(KD):
                def mm(t, k=k):
                    return t.matmul(pb[6][:], ones16[:], hsq16[:, TT * k:TT * (k + 1)],
                                    start=(k == 0), stop=(k == KD - 1))
                t_rms = op("tensor", mm, "tM") if k == KD - 1 else op("tensor", mm)
            t_rms_prev = t_rms
            wait("scalar", "tM", t_rms)
            op("scalar", lambda s: s.activation(vt2[:], pb[6][:], AF.Ln, scale=1.0 / D, bias=epsc[:, 0:1]))
            a_rms = op("scalar", lambda s: s.activation(rmsr[:], vt2[:], AF.Exp, scale=-0.5), "aA")
            a_rms_prev = a_rms
            a_sg = op("scalar", lambda s: s.activation(sg16[:], g16s[:], AF.Silu), "aA")
            wait("vector", "aA", a_sg)
            for k in range(KD):
                sl = slice(TT * k, TT * (k + 1))
                od = o16[:, TC * k + TT * n: TC * k + TT * (n + 1)]
                op("vector", lambda v, sl=sl: v.tensor_mul(hw[:, sl], hw[:, sl], rmsr[:]))
                op("vector", lambda v, sl=sl: v.tensor_mul(hw[:, sl], hw[:, sl], sg16[:, sl]))
                op("vector", lambda v, sl=sl, k=k, od=od: v.tensor_scalar(
                    od, hw[:, sl], gnwso[:, k:k + 1], None, ALU.mult))
            v_o16 = fence("vector")
        if DEBUG:
            wait("gpsimd", "vV", v_o16)
            for k in range(KD):
                dma("gpsimd", dbg_o16[128 * k:128 * (k + 1), :], o16[:, TC * k:TC * (k + 1)], "qG")
        p2a.close()
        barrier_dma()

        p2b = ExitStack()
        E2 = p2b.enter_context
        xres = E2(nc.sbuf_tensor("xres", [128, KD * TC], F32))
        wosl = [E2(nc.sbuf_tensor(f"wosl{i}", [128, KD * 512], F16)) for i in range(2)]
        x1t = E2(nc.sbuf_tensor("x1t", [128, TT], F32))

        q_xr = 0
        for k in range(KD):
            q_xr = dma("sync", xres[:, TC * k:TC * (k + 1)], x_in[128 * k:128 * (k + 1), :], "qS")
        f_xr = qfence("sync")

        def woslice(buf, k):
            return buf[:, k * 512:(k + 1) * 512]

        t_wog = []
        q_wog = []
        f_wog = []
        v_ev2 = []
        q_x1s = 0
        f_x1t = None
        for grp in range(4 + 1):
            if grp < 4:
                if grp >= 2:
                    wait("gpsimd", "tM", t_wog[grp - 2])
                qv = 0
                for k in range(KD):
                    qv = dma("gpsimd", woslice(wosl[grp % 2], k),
                             woT[128 * k:128 * (k + 1), 512 * grp:512 * (grp + 1)], "qG")
                q_wog.append(qv)
                f_wog.append(qfence("gpsimd"))
            if grp == 0:
                continue
            g = grp - 1
            wait("tensor", "qG", q_wog[g])
            wfence("tensor", f_wog[g])
            if g == 0:
                wait("tensor", "vV", v_o16)
            for mloc in range(4):
                m = 4 * g + mloc
                pair = m % 3
                if m >= 3:
                    wait("tensor", "vV", v_ev2[m - 3])
                lastmm = None
                for k in range(KD):
                    for n in range(NT):
                        def mm(t, k=k, n=n, mloc=mloc, g=g, pair=pair):
                            return t.matmul(pb[2 * pair + n][:],
                                            woslice(wosl[g % 2], k)[:, 128 * mloc:128 * (mloc + 1)],
                                            o16[:, TC * k + TT * n:TC * k + TT * (n + 1)],
                                            start=(k == 0), stop=(k == KD - 1))
                        if k == KD - 1 and n == NT - 1:
                            lastmm = op("tensor", mm, "tM")
                        else:
                            op("tensor", mm)
                if mloc == 3:
                    t_wog.append(lastmm)
                wait("vector", "tM", lastmm)
                if m == 0:
                    wait("vector", "qS", q_xr)
                    wfence("vector", f_xr)
                vlast = None
                for n in range(NT):
                    ps = pb[2 * pair + n]
                    if f_x1t is not None:
                        wfence("vector", f_x1t)
                    op("vector", lambda v, ps=ps: v.tensor_scalar(
                        x1t[:], ps[:], 256.0, MAGIC, ALU.mult, ALU.add))
                    op("vector", lambda v: v.tensor_scalar(
                        x1t[:], x1t[:], -MAGIC, 1.0 / 256.0, ALU.add, ALU.mult))
                    if DEBUG:
                        vdbg = fence("vector")
                        wait("gpsimd", "vV", vdbg)
                        qdbg = dma("gpsimd", dbg_att[128 * m:128 * (m + 1), TT * n:TT * (n + 1)], x1t[:], "qG")
                        wait("vector", "qG", qdbg)
                    vlast = op("vector", lambda v, m=m, n=n: v.scalar_tensor_tensor(
                        x1t[:], x1t[:], gac[:, m:m + 1],
                        xres[:, TC * m + TT * n:TC * m + TT * (n + 1)], ALU.mult, ALU.add), "vV")
                    wait("gpsimd", "vV", vlast)
                    q_x1s = dma("gpsimd", x1_sp[128 * m:128 * (m + 1), TT * n:TT * (n + 1)], x1t[:], "qG")
                    f_x1t = qfence("gpsimd")
                v_ev2.append(vlast)
        q_x1_spill = cnt["qG"]
        p2b.close()
        po.close()
        barrier_dma()

        pqx = ExitStack()
        qx = pqx.enter_context(nc.sbuf_tensor("qx", [128, KD * TC], BF16))
        p4a = ExitStack()
        E4 = p4a.enter_context
        x1f = E4(nc.sbuf_tensor("x1f", [128, KD * TT], F32))
        x1sq = E4(nc.sbuf_tensor("x1sq", [128, KD * TT], F32))
        h2f = E4(nc.sbuf_tensor("h2f", [128, KD * TT], F32))
        mrep2 = E4(nc.sbuf_tensor("mrep2", [128, TT], F32))
        rstd2 = E4(nc.sbuf_tensor("rstd2", [128, TT], F32))
        vtmp2 = E4(nc.sbuf_tensor("vtmp2", [128, TT], F32))
        ttmp2 = E4(nc.sbuf_tensor("ttmp2", [128, TT], F32))
        mx = E4(nc.sbuf_tensor("mx", [128, TT], F32))
        s1b = E4(nc.sbuf_tensor("s1b", [128, TT], F32))

        f_x1sp = qfence("gpsimd")
        wait("sync", "qG", q_x1_spill)
        wfence("sync", f_x1sp)
        v_qx = None
        t_ln2_prev = None
        a_sq2 = None
        v_var2 = None
        for n in range(NT):
            if n > 0:
                wait("sync", "vV", v_qx)
                wait("sync", "tM", t_ln2_prev)
            q_x = 0
            for k in range(KD):
                q_x = dma("sync", x1f[:, TT * k:TT * (k + 1)],
                          x1_sp[128 * k:128 * (k + 1), TT * n:TT * (n + 1)], "qS")
            f_x2 = qfence("sync")
            wait("scalar", "qS", q_x)
            wfence("scalar", f_x2)
            a_sq2 = op("scalar", lambda s: s.activation(x1sq[:], x1f[:], AF.Square), "aA")
            wait("tensor", "qS", q_x)
            wfence("tensor", f_x2)
            if n > 0:
                wait("tensor", "vV", v_var2)
            for k in range(KD):
                op("tensor", lambda t, k=k: t.matmul(
                    pb[6][:], ones32[:], x1f[:, TT * k:TT * (k + 1)],
                    start=(k == 0), stop=(k == KD - 1)))
            wait("tensor", "aA", a_sq2)
            t_ln2 = None
            for k in range(KD):
                def mm(t, k=k):
                    return t.matmul(pb[7][:], ones32[:], x1sq[:, TT * k:TT * (k + 1)],
                                    start=(k == 0), stop=(k == KD - 1))
                t_ln2 = op("tensor", mm, "tM") if k == KD - 1 else op("tensor", mm)
            t_ln2_prev = t_ln2
            wait("vector", "tM", t_ln2)
            op("vector", lambda v: v.tensor_scalar(mrep2[:], pb[6][:], 1.0 / D, None, ALU.mult))
            op("vector", lambda v: v.tensor_scalar(vtmp2[:], pb[7][:], 1.0 / D, None, ALU.mult))
            op("vector", lambda v: v.scalar_tensor_tensor(ttmp2[:], mrep2[:], -1.0, mrep2[:],
                                                          ALU.mult, ALU.mult))
            op("vector", lambda v: v.tensor_add(vtmp2[:], vtmp2[:], ttmp2[:]))
            v_var2 = fence("vector")
            wait("scalar", "vV", v_var2)
            op("scalar", lambda s: s.activation(ttmp2[:], vtmp2[:], AF.Ln, bias=epsc[:, 0:1]))
            a_rstd2 = op("scalar", lambda s: s.activation(rstd2[:], ttmp2[:], AF.Exp, scale=-0.5), "aA")
            wait("vector", "aA", a_rstd2)
            wait("vector", "qS", q_x)
            wfence("vector", f_x2)
            op("vector", lambda v: v.memset(mx[:], 0.0))
            for k in range(KD):
                sl = slice(TT * k, TT * (k + 1))
                op("vector", lambda v, sl=sl: v.tensor_sub(ttmp2[:], x1f[:, sl], mrep2[:]))
                op("vector", lambda v: v.tensor_mul(ttmp2[:], ttmp2[:], rstd2[:]))
                op("vector", lambda v, k=k, sl=sl: v.tensor_scalar(
                    h2f[:, sl], ttmp2[:], cb1[:, k:k + 1], sb_c[:, k:k + 1], ALU.mult, ALU.add))
                op("vector", lambda v, sl=sl: v.scalar_tensor_tensor(
                    ttmp2[:], h2f[:, sl], -1.0, h2f[:, sl], ALU.mult, ALU.max))
                op("vector", lambda v: v.tensor_max(mx[:], mx[:], ttmp2[:]))
            v_mx = fence("vector")
            part_max_row(mx, TT, 6, rowA, v_mx)
            op("vector", lambda v: v.tensor_scalar_max(rowB[0:1, 0:TT], rowA[0:1, 0:TT], 1e-5))
            op("vector", lambda v: v.drain())
            op("vector", lambda v, n=n: v.tensor_scalar(
                dqrow[0:1, TT * n:TT * (n + 1)], rowB[0:1, 0:TT], 1.0 / 127.0, None, ALU.mult))
            op("vector", lambda v: v.reciprocal(rowA[0:1, 0:TT], rowB[0:1, 0:TT]))
            op("vector", lambda v: v.drain())
            op("vector", lambda v: v.tensor_scalar(rowB[0:1, 0:TT], rowA[0:1, 0:TT], 127.0,
                                                   None, ALU.mult))
            op("vector", lambda v: v.drain())
            v_s1row = fence("vector")
            wait("tensor", "vV", v_s1row)
            t_s1b = op("tensor", lambda t: t.matmul(pb[6][:], ones1[:], rowB[0:1, 0:TT],
                                                    start=True, stop=True), "tM")
            wait("vector", "tM", t_s1b)
            op("vector", lambda v: v.tensor_copy(s1b[:], pb[6][:]))
            for k in range(KD):
                sl = slice(TT * k, TT * (k + 1))
                qd = qx[:, TC * k + TT * n:TC * k + TT * (n + 1)]
                op("vector", lambda v, sl=sl: v.tensor_mul(ttmp2[:], h2f[:, sl], s1b[:]))
                op("vector", lambda v, qd=qd: v.tensor_scalar(
                    qd, ttmp2[:], MAGIC, -MAGIC, ALU.add, ALU.add))
            v_qx = fence("vector")
        wait("tensor", "vV", v_qx)
        t_dq1 = op("tensor", lambda t: t.matmul(pb[6][:], ones1[:], dqrow[0:1, 0:TT],
                                                start=True, stop=True), "tM")
        t_dq2 = op("tensor", lambda t: t.matmul(pb[7][:], ones1[:], dqrow[0:1, TT:TC],
                                                start=True, stop=True), "tM")
        wait("vector", "tM", t_dq2)
        op("vector", lambda v: v.tensor_copy(dqb[:, 0:TT], pb[6][:]))
        op("vector", lambda v: v.tensor_copy(dqb[:, TT:TC], pb[7][:]))
        v_dqb = fence("vector")
        if DEBUG:
            wait("gpsimd", "vV", v_dqb)
            for k in range(KD):
                dma("gpsimd", dbg_qx[128 * k:128 * (k + 1), :], qx[:, TC * k:TC * (k + 1)], "qG")
            dma("gpsimd", dbg_rows[0:1, :], dqrow[:], "qG")
        p4a.close()
        barrier_dma()

        p4b = ExitStack()
        E4 = p4b.enter_context
        wt = [E4(nc.sbuf_tensor(f"wt4{i}", [128, 2 * KD * 512], BF16)) for i in range(2)]
        gu32 = E4(nc.sbuf_tensor("gu32", [128, TC], F32))
        silu16 = E4(nc.sbuf_tensor("silu16", [128, TC], F16))
        up32 = E4(nc.sbuf_tensor("up32", [128, TC], F32))
        h16r = [E4(nc.sbuf_tensor(f"h16r{i}", [128, TC], F16)) for i in range(2)]
        hq16r = [E4(nc.sbuf_tensor(f"hq16r{i}", [128, TC], F16)) for i in range(2)]
        mx2a = E4(nc.sbuf_tensor("mx2a", [128, TC], F32))
        mx2b = E4(nc.sbuf_tensor("mx2b", [128, TC], F32))
        op("vector", lambda v: v.memset(mx2a[:], 0.0))

        NSEC = 16

        def wt_slice(buf, w, k):
            base = (w * KD + k) * 512
            return buf[:, base:base + 512]

        wait("sync", "cC", c_ag2u)
        t_sec = []
        q_sec = []
        f_sec = []
        ev_fence = []
        a_silu_prev = None
        a_hsq_prev = None
        t_statsm_prev = None
        pend_stats = []
        stats_done = {}
        v_mx2_prev = None
        q_hsp_prev = 0
        pair_step = 0
        t_stats_last = None
        mx_cur, mx_nxt = mx2a, mx2b
        for sec in range(NSEC + 1):
            if sec < NSEC:
                if sec >= 2:
                    wait("sync", "tM", t_sec[sec - 2])
                qv = 0
                for w, wdram in enumerate((tgout, tuout)):
                    for k in range(KD):
                        qv = dma("sync", wt_slice(wt[sec % 2], w, k),
                                 wdram.ap()[128 * k:128 * (k + 1), 512 * sec:512 * (sec + 1)], "qS")
                q_sec.append(qv)
                f_sec.append(qfence("sync"))
            if sec == 0:
                continue
            s_ = sec - 1
            wait("tensor", "qS", q_sec[s_])
            wfence("tensor", f_sec[s_])
            if s_ == 0:
                wait("tensor", "vV", v_dqb)
            for mloc in range(4):
                m = 4 * s_ + mloc
                gate_pair = pair_step % 3
                if pair_step >= 3:
                    wait("tensor", "vV", ev_fence[pair_step - 3])
                if len(pend_stats) >= 2:
                    pm, phq, pahsq = pend_stats.pop(0)
                    wait("tensor", "aA", pahsq)
                    for n in range(NT):
                        def mm(t, n=n, pm=pm, phq=phq):
                            return t.matmul(pb[6 + n][:], ones16[:],
                                            phq[:, TT * n:TT * (n + 1)],
                                            start=(pm == 0), stop=(pm == KI - 1))
                        t_statsm_prev = op("tensor", mm, "tM") if n == NT - 1 else op("tensor", mm)
                    stats_done[pm] = t_statsm_prev
                lastmm = None
                for k in range(KD):
                    for n in range(NT):
                        def mm(t, k=k, n=n, mloc=mloc, s_=s_, pair=gate_pair):
                            return t.matmul(
                                pb[2 * pair + n][:],
                                wt_slice(wt[s_ % 2], 0, k)[:, 128 * mloc:128 * (mloc + 1)],
                                qx[:, TC * k + TT * n:TC * k + TT * (n + 1)],
                                start=(k == 0), stop=(k == KD - 1))
                        if k == KD - 1 and n == NT - 1:
                            lastmm = op("tensor", mm, "tM")
                        else:
                            op("tensor", mm)
                t_gate = lastmm
                pair_step += 1
                up_pair = pair_step % 3
                if pair_step >= 3:
                    wait("tensor", "vV", ev_fence[pair_step - 3])
                for k in range(KD):
                    for n in range(NT):
                        def mm(t, k=k, n=n, mloc=mloc, s_=s_, pair=up_pair):
                            return t.matmul(
                                pb[2 * pair + n][:],
                                wt_slice(wt[s_ % 2], 1, k)[:, 128 * mloc:128 * (mloc + 1)],
                                qx[:, TC * k + TT * n:TC * k + TT * (n + 1)],
                                start=(k == 0), stop=(k == KD - 1))
                        if k == KD - 1 and n == NT - 1:
                            lastmm = op("tensor", mm, "tM")
                        else:
                            op("tensor", mm)
                t_up = lastmm
                pair_step += 1
                wait("vector", "tM", t_gate)
                wait("vector", "aA", a_silu_prev)
                for n in range(NT):
                    op("vector", lambda v, n=n, p=gate_pair: v.scalar_tensor_tensor(
                        gu32[:, TT * n:TT * (n + 1)], pb[2 * p + n][:], swv[:, 3:4],
                        dqb[:, TT * n:TT * (n + 1)], ALU.mult, ALU.mult))
                ev_fence.append(fence("vector"))
                wait("scalar", "vV", ev_fence[-1])
                if m >= 2 and (m - 2) in stats_done:
                    wait("scalar", "tM", stats_done[m - 2])
                wait("scalar", "vV", v_mx2_prev)
                a_silu = op("scalar", lambda s: s.activation(silu16[:], gu32[:], AF.Silu), "aA")
                a_silu_prev = a_silu
                wait("vector", "tM", t_up)
                wait("vector", "aA", a_hsq_prev)
                if q_hsp_prev and m % 2 == 0:
                    wait("vector", "qG", q_hsp_prev)
                    wfence("vector", qfence("gpsimd"))
                h16 = h16r[m % 2]
                hq16 = hq16r[m % 2]
                for n in range(NT):
                    op("vector", lambda v, n=n, p=up_pair: v.scalar_tensor_tensor(
                        up32[:, TT * n:TT * (n + 1)], pb[2 * p + n][:], swv[:, 4:5],
                        dqb[:, TT * n:TT * (n + 1)], ALU.mult, ALU.mult))
                wait("vector", "aA", a_silu)
                op("vector", lambda v, h16=h16: v.tensor_mul(h16[:], silu16[:], up32[:]))
                v_h16 = fence("vector")
                ev_fence.append(v_h16)
                wait("scalar", "vV", v_h16)
                a_hsq = op("scalar", lambda s, h16=h16, hq16=hq16: s.activation(hq16[:], h16[:], AF.Square), "aA")
                a_hsq_prev = a_hsq
                pend_stats.append((m, hq16, a_hsq))
                wait("vector", "aA", a_hsq)
                v_mx2_prev = op("vector", lambda v, m=m, a=mx_cur, b=mx_nxt, hq16=hq16: v.scalar_tensor_tensor(
                    b[:], hq16[:], lnw2[:, m:m + 1], a[:], ALU.mult, ALU.max), "vV")
                mx_cur, mx_nxt = mx_nxt, mx_cur
                wait("gpsimd", "vV", v_mx2_prev)
                q_hsp_prev = dma("gpsimd", hf_sp[128 * m:128 * (m + 1), :], h16[:], "qG")
            if mloc == 3:
                t_sec.append(t_up)
        while pend_stats:
            pm, phq, pahsq = pend_stats.pop(0)
            wait("tensor", "aA", pahsq)
            for n in range(NT):
                def mm(t, n=n, pm=pm, phq=phq):
                    return t.matmul(pb[6 + n][:], ones16[:],
                                    phq[:, TT * n:TT * (n + 1)],
                                    start=(pm == 0), stop=(pm == KI - 1))
                t_statsm_prev = op("tensor", mm, "tM") if n == NT - 1 else op("tensor", mm)
            stats_done[pm] = t_statsm_prev
        t_stats_last = t_statsm_prev
        q_hf_spill = cnt["qG"]
        f_hfsp = qfence("gpsimd")
        mx2_final = mx_cur

        rbrow = E4(nc.sbuf_tensor("rbrow", [1, TC], F32))
        wait("scalar", "tM", t_stats_last)
        for n in range(NT):
            op("scalar", lambda s, n=n: s.activation(rowA[0:1, TT * n:TT * (n + 1)],
                                                     pb[6 + n][0:1, :], AF.Ln, scale=1.0 / I, bias=epsc[0:1, 1:2]))
        a_rb = op("scalar", lambda s: s.activation(rbrow[0:1, :], rowA[0:1, :], AF.Exp, scale=-0.5), "aA")
        v_mx2_mark = v_mx2_prev
        part_max_row(mx2_final, TC, 5, rowB, v_mx2_mark)
        v_pm = fence("vector")
        wait("scalar", "vV", v_pm)
        a_m2 = op("scalar", lambda s: s.activation(rowB[0:1, :], rowB[0:1, :], AF.Sqrt), "aA")
        wait("vector", "aA", max(a_rb, a_m2))
        op("vector", lambda v: v.tensor_mul(rowB[0:1, :], rowB[0:1, :], rbrow[0:1, :]))
        op("vector", lambda v: v.tensor_scalar_max(rowB[0:1, :], rowB[0:1, :], 1e-5))
        op("vector", lambda v: v.drain())
        op("vector", lambda v: v.tensor_scalar(dq2row[0:1, :], rowB[0:1, :], 1.0 / 127.0, None, ALU.mult))
        op("vector", lambda v: v.reciprocal(rowA[0:1, :], rowB[0:1, :]))
        op("vector", lambda v: v.drain())
        op("vector", lambda v: v.tensor_scalar(rowB[0:1, :], rowA[0:1, :], 127.0, None, ALU.mult))
        op("vector", lambda v: v.drain())
        op("vector", lambda v: v.tensor_mul(c2row[0:1, :], rowB[0:1, :], rbrow[0:1, :]))
        op("vector", lambda v: v.drain())
        v_rows = fence("vector")
        if DEBUG:
            wait("gpsimd", "vV", v_rows)
            dma("gpsimd", dbg_r2[3:4, 0:8], swv[0:1, 0:8], "qG")
            dma("gpsimd", dbg_r2[0:1, :], c2row[:], "qG")
            dma("gpsimd", dbg_r2[1:2, :], dq2row[:], "qG")
            dma("gpsimd", dbg_r2[2:3, :], rbrow[:], "qG")
        wait("tensor", "vV", v_rows)
        for bi, (rsrc, bdst) in enumerate(((c2row, c2b), (dq2row, dq2b))):
            for j in range(NT):
                t_b = op("tensor", lambda t, r=rsrc, j=j, bi=bi: t.matmul(
                    pb[2 * bi + j][:], ones1[:], r[0:1, TT * j:TT * (j + 1)],
                    start=True, stop=True), "tM")
                wait("vector", "tM", t_b)
                op("vector", lambda v, b=bdst, j=j, bi=bi: v.tensor_copy(
                    b[:, TT * j:TT * (j + 1)], pb[2 * bi + j][:]))
        v_bc2 = fence("vector")
        p4b.close()
        pqx.close()
        barrier_dma()

        p5 = ExitStack()
        E5 = p5.enter_context
        qh = E5(nc.sbuf_tensor("qh", [128, KI * TT], BF16))
        h16b = [E5(nc.sbuf_tensor(f"h16b{i}", [128, 8 * TT], F16)) for i in range(2)]
        qtmp = E5(nc.sbuf_tensor("qtmp", [128, TT], F32))
        wd_t = [E5(nc.sbuf_tensor(f"wd{i}", [128, 8 * 1024], BF16)) for i in range(2)]
        x1c = E5(nc.sbuf_tensor("x1c", [128, 8 * TT], F32))
        ffc = [E5(nc.sbuf_tensor(f"ffc{i}", [128, TT], F32)) for i in range(4)]

        wait("sync", "cC", c_ag2d)
        wait("sync", "qG", q_hf_spill)
        wfence("sync", f_hfsp)
        t_dn_last = None
        v_outev = None
        out_marks = []
        estep = 0
        for n in range(NT):
            v_qh_marks = []
            NB = 8
            for b in range(KI // NB):
                if b >= 2:
                    wait("sync", "vV", v_qh_marks[(b - 2) * NB + NB - 1])
                qld = 0
                for kk in range(NB):
                    k = b * NB + kk
                    qld = dma("sync", h16b[b % 2][:, TT * kk:TT * (kk + 1)],
                              hf_sp[128 * k:128 * (k + 1), TT * n:TT * (n + 1)], "qS")
                f_hb = qfence("sync")
                wait("vector", "qS", qld)
                wfence("vector", f_hb)
                if b == 0 and n == 0:
                    wait("vector", "vV", v_bc2)
                if b == 0 and n == 1:
                    wait("vector", "tM", t_dn_last)
                for kk in range(NB):
                    k = b * NB + kk
                    op("vector", lambda v, k=k, kk=kk, b=b, n=n: v.scalar_tensor_tensor(
                        qtmp[:], h16b[b % 2][:, TT * kk:TT * (kk + 1)], lnwv[:, k:k + 1],
                        c2b[:, TT * n:TT * (n + 1)], ALU.mult, ALU.mult))
                    v_qh_marks.append(op("vector", lambda v, k=k: v.tensor_scalar(
                        qh[:, TT * k:TT * (k + 1)], qtmp[:], MAGIC, -MAGIC, ALU.add, ALU.add), "vV"))
            v_qh = v_qh_marks[-1]
            if DEBUG and n == 0:
                wait("gpsimd", "vV", v_qh)
                for k in range(KI):
                    dma("gpsimd", dbg_qh[128 * k:128 * (k + 1), :], qh[:, TT * k:TT * (k + 1)], "qG")
            for grp in range(2):
                NBW = 8
                tM_base_d = cnt["tM"]
                q_wd = []
                f_wd = []
                for b in range(KI // NBW):
                    if b >= 2:
                        wait("sync", "tM", tM_base_d + (b - 1) * NBW)
                    qv = 0
                    for kk in range(NBW):
                        k = b * NBW + kk
                        qv = dma("sync", wd_t[b % 2][:, 1024 * kk:1024 * (kk + 1)],
                                 tdout.ap()[128 * k:128 * (k + 1),
                                            1024 * grp:1024 * (grp + 1)], "qS")
                    q_wd.append(qv)
                    f_wd.append(qfence("sync"))
                if v_outev is not None:
                    wait("tensor", "vV", v_outev)
                lastmm = None
                for b in range(KI // NBW):
                    wait("tensor", "vV", v_qh_marks[b * NBW + NBW - 1])
                    wait("tensor", "qS", q_wd[b])
                    wfence("tensor", f_wd[b])
                    for kk in range(NBW):
                        k = b * NBW + kk
                        for mloc in range(8):
                            def mm(t, k=k, kk=kk, b=b, mloc=mloc):
                                return t.matmul(pb[mloc][:],
                                                wd_t[b % 2][:, 1024 * kk + 128 * mloc:1024 * kk + 128 * (mloc + 1)],
                                                qh[:, TT * k:TT * (k + 1)],
                                                start=(k == 0), stop=(k == KI - 1))
                            if mloc == 7:
                                lastmm = op("tensor", mm, "tM")
                            else:
                                op("tensor", mm)
                t_dn_last = lastmm
                wait("vector", "tM", t_dn_last)
                q_x1c = 0
                for mloc in range(8):
                    m = 8 * grp + mloc
                    q_x1c = dma("sync", x1c[:, TT * mloc:TT * (mloc + 1)],
                                x1_sp[128 * m:128 * (m + 1), TT * n:TT * (n + 1)], "qS")
                f_x1c = qfence("sync")
                wait("vector", "qS", q_x1c)
                wfence("vector", f_x1c)
                for mloc in range(8):
                    m = 8 * grp + mloc
                    if estep >= 4:
                        wait("vector", "qS", out_marks[estep - 4])
                        if estep % 4 == 0:
                            wfence("vector", qfence("sync"))
                    fb = ffc[estep % 4]
                    op("vector", lambda v, mloc=mloc, n=n, fb=fb: v.scalar_tensor_tensor(
                        fb[:], pb[mloc][:], swv[:, 5:6], dq2b[:, TT * n:TT * (n + 1)],
                        ALU.mult, ALU.mult))
                    v_outev = op("vector", lambda v, m=m, mloc=mloc, fb=fb: v.scalar_tensor_tensor(
                        fb[:], fb[:], gbc[:, m:m + 1], x1c[:, TT * mloc:TT * (mloc + 1)],
                        ALU.mult, ALU.add), "vV")
                    wait("sync", "vV", v_outev)
                    out_marks.append(dma("sync", out_t[128 * m:128 * (m + 1), TT * n:TT * (n + 1)],
                                         fb[:], "qS"))
                    estep += 1
        q_final = cnt["qS"]
        p5.close()

        wait("sync", "qS", q_final)
        wait("gpsimd", "qG", cnt["qG"])

        block = E(nc.Block())

        @block.sync
        def _(e):
            for fn in plan["sync"]:
                fn(e)

        @block.gpsimd
        def _(e):
            for fn in plan["gpsimd"]:
                fn(e)

        @block.tensor
        def _(e):
            for fn in plan["tensor"]:
                fn(e)

        @block.vector
        def _(e):
            for fn in plan["vector"]:
                fn(e)

        @block.scalar
        def _(e):
            for fn in plan["scalar"]:
                fn(e)

    return nc


def _prep_inputs(inputs):
    f32 = np.float32
    x = np.asarray(inputs["L0_attn_hidden_states"])[0]
    xT = np.ascontiguousarray(x.T)
    wi = np.ascontiguousarray(np.asarray(inputs["L0_attn_w_i"], dtype=f32))
    wf = np.ascontiguousarray(np.asarray(inputs["L0_attn_w_f"], dtype=f32))
    wg = np.ascontiguousarray(np.asarray(inputs["L0_attn_w_g"], dtype=f32))
    woT = np.ascontiguousarray(np.asarray(inputs["L0_attn_w_o"]).T.astype(f32))
    wgT = np.ascontiguousarray(np.asarray(inputs["L0_ffn_w_gate"]).T.astype(f32))
    wuT = np.ascontiguousarray(np.asarray(inputs["L0_ffn_w_up"]).T.astype(f32))
    wdT = np.ascontiguousarray(np.asarray(inputs["L0_ffn_w_down"]).T.astype(f32))
    wmod = np.asarray(inputs["L0_wmod"], dtype=f32)

    def pk(v, K):
        return np.ascontiguousarray(np.asarray(v, dtype=f32).reshape(K, 128).T)

    base = {
        "wi": wi, "wf": wf, "wg": wg, "woT": woT,
        "c_pk": pk(np.asarray(inputs["L0_c"]).reshape(-1), KD),
        "si": pk(inputs["L0_attn_s_i"], KD), "sf": pk(inputs["L0_attn_s_f"], KD),
        "sg": pk(inputs["L0_attn_s_g"], KD), "so": pk(inputs["L0_attn_s_o"], KD),
        "gnw": pk(inputs["L0_attn_g_norm_w"], KD),
        "lnw": pk(inputs["L0_ffn_ln_weight"], KI),
        "ident": np.eye(128, dtype=f32),
    }
    in_maps = []
    for c in range(NC):
        m = dict(base)
        m["x_in"] = np.ascontiguousarray(xT[:, TC * c:TC * (c + 1)])
        m["wmod_sl"] = np.ascontiguousarray(wmod[:, MODS * c:MODS * (c + 1)])
        m["tsl_g"] = np.ascontiguousarray(wgT[(D // NC) * c:(D // NC) * (c + 1), :])
        m["tsl_u"] = np.ascontiguousarray(wuT[(D // NC) * c:(D // NC) * (c + 1), :])
        m["tsl_d"] = np.ascontiguousarray(wdT[(I // NC) * c:(I // NC) * (c + 1), :])
        cm0v = np.zeros((128, NC), f32)
        cm0v[:, :c] = 1.0
        m["cm0"] = np.ascontiguousarray(cm0v)
        m["cm1"] = np.ascontiguousarray(1.0 - cm0v)
        in_maps.append(m)
    return in_maps


def kernel(**inputs):
    if "nc" not in _CACHED:
        _CACHED["nc"] = build_nc()
    nc = _CACHED["nc"]
    in_maps = _prep_inputs(inputs)
    res = run_bass_kernel_spmd(nc, in_maps, core_ids=list(range(NC)))
    outs = [res.results[c]["out_t"] for c in range(NC)]
    full = np.concatenate(outs, axis=1)
    return np.ascontiguousarray(full.T)[None].astype(np.float32)
